# version 18
# baseline (speedup 1.0000x reference)
import numpy as np
import ml_dtypes
from contextlib import ExitStack
import concourse.bass as bass
import concourse.mybir as mybir
from concourse.bass_utils import run_bass_kernel_spmd

F32 = mybir.dt.float32
BF16 = mybir.dt.bfloat16
AF = mybir.ActivationFunctionType
ALU = mybir.AluOpType
AX = mybir.AxisListType

B, L, E, H, D = 2, 16384, 1024, 16, 64
NCORES = 8
T = 4096
TT = 256
NTILE = T // TT
C = 128
DEBUG_TAPS = ()
TAP_T = 0

CTX_BANK = [h // 6 for h in range(16)]
CTX_COL = [65 * (h % 6) for h in range(16)]


def build_program(a_tiles=NTILE, exchange=True, b_tiles=NTILE):
    nc = bass.Bass("TRN2", target_bir_lowering=False, debug=False)

    x_h = nc.dram_tensor("x", [T, E], F32, kind="ExternalInput").ap()
    qkv_h = nc.dram_tensor("qkv_w", [E, 3 * E], F32, kind="ExternalInput").ap()
    bq_h = nc.dram_tensor("bq_cols", [128, 8], F32, kind="ExternalInput").ap()
    bk_h = nc.dram_tensor("bk_cols", [128, 8], F32, kind="ExternalInput").ap()
    vb_h = nc.dram_tensor("vbias_bc", [128, E], F32, kind="ExternalInput").ap()
    wo_h = nc.dram_tensor("out_w", [E, E], F32, kind="ExternalInput").ap()
    ob_h = nc.dram_tensor("outb_bc", [128, E], F32, kind="ExternalInput").ap()
    pj_h = nc.dram_tensor("projbd", [8, 128, 128], F32, kind="ExternalInput").ap()
    fw_h = nc.dram_tensor("ffn_wT", [E, E], F32, kind="ExternalInput").ap()
    sc_h = nc.dram_tensor("scale_bc", [128, E], F32, kind="ExternalInput").ap()
    fb_h = nc.dram_tensor("bias_bc", [128, E], F32, kind="ExternalInput").ap()
    mk_h = nc.dram_tensor("mask4", [128, 512], F32, kind="ExternalInput").ap()
    id_h = nc.dram_tensor("ident", [128, 128], F32, kind="ExternalInput").ap()
    rm_h = nc.dram_tensor("rankmask", [128, NCORES], F32, kind="ExternalInput").ap()
    y_h = nc.dram_tensor("y", [T, E], F32, kind="ExternalOutput").ap()

    sp_xn = nc.dram_tensor("sp_xn", [NTILE, 128, 2048], BF16).ap()
    sp_kp = nc.dram_tensor("sp_kp", [NTILE, 128, 2048], BF16).ap()
    sp_kt = nc.dram_tensor("sp_kt", [NTILE, 128, 2048], BF16).ap()
    sp_va = nc.dram_tensor("sp_va", [NTILE, 128, 2080], BF16).ap()
    cc_in = nc.dram_tensor("cc_in", [128, 520], BF16).ap()
    cc_out = nc.dram_tensor("cc_out", [NCORES * 128, 520], BF16,
                            addr_space="Shared").ap()

    sb = nc.alloc_sbuf_tensor
    wq_sb = sb("wq_sb", [128, 8 * 1024], BF16).ap()
    wk_sb = sb("wk_sb", [128, 8 * 1024], BF16).ap()
    wv_sb = sb("wv_sb", [128, 8 * 1024], BF16).ap()
    wo_sb = sb("wo_sb", [128, 8 * 1024], BF16).ap()
    ws_sb = sb("ws_sb", [128, 8 * 1024], BF16).ap()
    pj_sb = sb("pj_sb", [128, 8 * 128], BF16).ap()
    bq_sb = sb("bq_sb", [128, 8], F32).ap()
    bk_sb = sb("bk_sb", [128, 8], F32).ap()
    vb_sb = sb("vb_sb", [128, E], F32).ap()
    ob_sb = sb("ob_sb", [128, E], F32).ap()
    sc_sb = sb("sc_sb", [128, E], F32).ap()
    fb_sb = sb("fb_sb", [128, E], F32).ap()
    mk_sb = sb("mk_sb", [128, 512], F32).ap()
    id_sb = sb("id_sb", [128, 128], BF16).ap()
    rm_sb = sb("rm_sb", [128, NCORES], F32).ap()
    cst = sb("cst", [128, 2], F32).ap()
    x2 = [sb(f"x2_{i}", [128, 2048], F32).ap() for i in range(2)]
    xnfm2 = [sb(f"xnfm_{i}", [128, 2048], BF16).ap() for i in range(2)]
    kpfm2 = [sb(f"kpfm_{i}", [128, 2048], BF16).ap() for i in range(2)]
    ktm2 = [sb(f"ktm_{i}", [128, 2048], BF16).ap() for i in range(2)]
    vaug2 = [sb(f"vaug_{i}", [128, 2080], BF16).ap() for i in range(2)]
    kq_fm0 = sb("kq_fm", [128, 2048], BF16).ap()
    xn_tm0 = sb("xn_tm", [128, 2048], BF16).ap()
    xn_tm1 = wq_sb[:, 0:2048]
    kq_fm1 = wq_sb[:, 2048:4096]
    xn_tm2 = [xn_tm0, xn_tm1]
    kq_fm2 = [kq_fm0, kq_fm1]
    qp_top = sb("qp_top", [128, 2048], BF16).ap()
    qp_bot = sb("qp_bot", [128, 2048], BF16).ap()
    amt2 = [sb(f"amt_{i}", [128, 2048], BF16).ap() for i in range(2)]
    ctxn = sb("ctxn", [128, 2048], BF16).ap()
    ctxfm = sb("ctxfm", [128, 2048], BF16).ap()
    s_sb = sb("s_sb", [128, 2048], F32).ap()
    g_sb = sb("g_sb", [128, 2048], F32).ap()
    S_run = sb("S_run", [128, 520], F32).ap()
    S_bf = sb("S_bf", [128, 520], BF16).ap()
    Sg = sb("Sg", [128, 520], BF16).ap()
    lnscr = sb("lnscr", [128, 1024], F32).ap()
    ln_sb = sb("ln_sb", [128, 32], F32).ap()
    nrm_sb = sb("nrm_sb", [128, 64], F32).ap()

    ps = nc.alloc_psum_tensor
    pm = [ps(f"pm{i}", [128, 512], F32).ap() for i in range(3)]
    pc = [ps(f"pc{i}", [128, 512], F32).ap() for i in range(3)]
    pg = [ps(f"pg{i}", [128, 512], F32).ap() for i in range(2)]
    pm_bf = [p.bitcast(BF16) for p in pm]

    ENG = ("sync", "tensor", "vector", "scalar", "gpsimd")

    class Plan:
        def __init__(self):
            self.q = {e: [] for e in ENG}
            self.cnt = {}
            self.hdl = {}
            self.seen = {e: {} for e in ENG}
            self.r = {}

        def add_sem(self, name, hdl):
            self.hdl[name] = hdl
            self.cnt[name] = 0

        def _wait(self, eng, ev):
            sem, val = ev
            if val <= 0 or self.seen[eng].get(sem, 0) >= val:
                return
            self.seen[eng][sem] = val
            h = self.hdl[sem]
            epts = {"sync": nc.sync, "tensor": nc.tensor, "vector": nc.vector,
                    "scalar": nc.scalar, "gpsimd": nc.gpsimd}
            self.q[eng].append(lambda h=h, v=val, E=epts[eng]: E.wait_ge(h, v))

        def _flat(self, evs, out):
            for ev in evs:
                if ev is None:
                    continue
                if isinstance(ev, tuple) and len(ev) == 2 and isinstance(ev[0], str):
                    out.append(ev)
                else:
                    self._flat(ev, out)
            return out

        def op(self, eng, fn, inc=None, waits=(), key=None, note=None):
            for ev in self._flat(waits, []):
                self._wait(eng, ev)
            if inc is not None:
                sem, delta = inc
                self.cnt[sem] += delta
                val = self.cnt[sem]
                h = self.hdl[sem]

                def run(fn=fn, h=h, d=delta, note=note):
                    ins = fn()
                    if note:
                        ins.annotate(note)
                    return ins.then_inc(h, d)
                self.q[eng].append(run)
                ev = (sem, val)
                if key is not None:
                    self.r[key] = ev
                return ev
            else:
                self.q[eng].append(fn)
                return None

    P = Plan()

    stack = ExitStack()
    SEMS = ["T", "V", "S", "P", "CC", "CON", "WL", "FSL", "SGS", "CCS", "YO"]
    for st in ("LX", "LN", "LP", "LT", "LV", "SX", "SP", "ST", "SV"):
        SEMS += [st + "0", st + "1"]
    for i in range(8):
        SEMS.append(f"TP{i}")
    for name in SEMS:
        P.add_sem(name, stack.enter_context(nc.semaphore(name)))
    tap_n = [0]

    def rdy(key):
        return P.r.get(key)

    tn, vc, sc_e, gp, sy = nc.tensor, nc.vector, nc.scalar, nc.gpsimd, nc.sync

    def tap(name, shape, dtype, src_ap, waits):
        if name not in DEBUG_TAPS:
            return None
        t_h = nc.dram_tensor("tap_" + name, shape, dtype, kind="ExternalOutput").ap()
        sem = f"TP{tap_n[0]}"
        tap_n[0] += 1
        return P.op("sync", lambda t_h=t_h, src=src_ap: sy.dma_start(
            out=t_h[:, :], in_=src), inc=(sem, 16), waits=tuple(waits))

    class Rot:
        def __init__(self):
            self.i = 0
            self.banks = [0, 1, 2]
            self.free = {}

        def take(self):
            j = self.banks[self.i % len(self.banks)]
            self.i += 1
            return j, self.free.get(j)

        def give(self, j, ev):
            self.free[j] = ev

    rot = Rot()
    PMX = pm + pc
    PMX_BF = pm_bf + [p.bitcast(BF16) for p in pc]

    def mm_seq(jobs, wait_first, key):
        n = len(jobs)
        tag = str(key[0]) if isinstance(key, tuple) else str(key)
        for i, (o, l, r, st_, sp_, tp) in enumerate(jobs):
            w = tuple(wait_first) if i == 0 else ()
            P.op("tensor",
                 lambda o=o, l=l, r=r, st=st_, sp=sp_, tp=tp: tn.matmul(
                     o, l, r, start=st, stop=sp, tile_position=tp),
                 inc=("T", 1) if i == n - 1 else None, waits=w,
                 key=key if i == n - 1 else None, note=tag)
        return P.r.get(key)

    for dst, src in ((mk_sb, mk_h), (bq_sb, bq_h), (bk_sb, bk_h), (vb_sb, vb_h),
                     (ob_sb, ob_h), (sc_sb, sc_h), (fb_sb, fb_h), (rm_sb, rm_h)):
        P.op("sync", lambda d=dst, s=src: sy.dma_start(out=d[:, :], in_=s[:, :]),
             inc=("CON", 16))
    ev_consts = ("CON", P.cnt["CON"])

    P.op("gpsimd", lambda: gp.dma_start(out=id_sb[:, :], in_=id_h[:, :]),
         inc=("WL", 16))
    P.op("gpsimd", lambda: gp.dma_start(
        out=pj_sb.rearrange("p (j c) -> p j c", j=8), in_=pj_h.transpose([1, 0, 2])),
        inc=("WL", 16))
    for ic in range(8):
        P.op("gpsimd", lambda ic=ic: gp.dma_start(
            out=wk_sb[:, ic * 1024:(ic + 1) * 1024],
            in_=qkv_h[ic * 128:(ic + 1) * 128, 1024:2048]), inc=("WL", 16))
    for ic in range(8):
        P.op("gpsimd", lambda ic=ic: gp.dma_start(
            out=wv_sb[:, ic * 1024:(ic + 1) * 1024],
            in_=qkv_h[ic * 128:(ic + 1) * 128, 2048:3072]), inc=("WL", 16))
    for ic in range(8):
        P.op("gpsimd", lambda ic=ic: gp.dma_start(
            out=wo_sb[:, ic * 1024:(ic + 1) * 1024],
            in_=wo_h[ic * 128:(ic + 1) * 128, :]), inc=("WL", 16))
    ev_wload = ("WL", P.cnt["WL"])

    for ic in range(8):
        evg = P.op("gpsimd", lambda ic=ic: gp.dma_start(
            out=lnscr[:, :], in_=fw_h[ic * 128:(ic + 1) * 128, :]),
            inc=("FSL", 16), waits=(rdy(("sgn", ic - 1)),))
        ev1 = P.op("vector", lambda: vc.tensor_scalar(
            out=lnscr[:, :], in0=lnscr[:, :], scalar1=0.0, scalar2=None,
            op0=ALU.is_ge), inc=("V", 1), waits=(evg,))
        P.op("vector", lambda ic=ic: vc.tensor_scalar(
            out=ws_sb[:, ic * 1024:(ic + 1) * 1024], in0=lnscr[:, :],
            scalar1=2.0, scalar2=-1.0, op0=ALU.mult, op1=ALU.add),
            inc=("V", 1), waits=(ev1,), key=("sgn", ic))
    ev_sgn = rdy(("sgn", 7))

    P.op("vector", lambda: vc.memset(S_run[:, :], 0.0), inc=("V", 1),
         key=("SrunA",))
    P.op("vector", lambda: vc.memset(qp_top[64:128, :], 0.0), inc=("V", 1),
         key=("qpz", 0))
    P.op("vector", lambda: vc.memset(qp_bot[0:64, :], 0.0), inc=("V", 1),
         key=("qpz", 1))
    P.op("vector", lambda: vc.memset(cst[:, 0:1], 1e-5), inc=("V", 1))
    P.op("vector", lambda: vc.memset(cst[:, 1:2], 1e-6), inc=("V", 1))
    for i in range(2):
        P.op("vector", lambda i=i: vc.memset(
            vaug2[i].rearrange("p (t h d) -> p t h d", t=2, h=16)[:, :, :, 64:65],
            1.0), inc=("V", 1), key=("vones", i))
    ev_setup_v = ("V", P.cnt["V"])

    def emit_ln(t):
        bi = t % 2
        lo = 16 * bi
        ev_x = P.op("sync", lambda t=t, bi=bi: sy.dma_start(
            out=x2[bi].rearrange("p (s e) -> p s e", s=2),
            in_=x_h[t * TT:(t + 1) * TT, :].rearrange("(s p) e -> p s e", p=128)),
            inc=(f"LX{bi}", 16), waits=(rdy(("xfree", t - 2)),), key=("xA", t))
        ev_xreads = []
        ev_sq = {}
        ev_su = {}
        for s in range(2):
            sl = slice(s * 1024, (s + 1) * 1024)
            ev_sq[s] = P.op("scalar", lambda sl=sl, s=s, bi=bi, lo=lo:
                            sc_e.activation(
                                out=lnscr[:, :], in_=x2[bi][:, sl], func=AF.Square,
                                accum_out=ln_sb[:, lo + s:lo + s + 1]),
                            inc=("S", 1), waits=(ev_x, ev_sgn))
            ev_su[s] = P.op("vector", lambda sl=sl, s=s, bi=bi, lo=lo: vc.reduce_sum(
                out=ln_sb[:, lo + 2 + s:lo + 3 + s], in_=x2[bi][:, sl], axis=AX.X),
                inc=("V", 1), waits=(ev_x, ev_setup_v))
            ev_xreads.append(ev_su[s])
        ev_mn = P.op("vector", lambda lo=lo: vc.tensor_scalar(
            out=ln_sb[:, lo + 4:lo + 6], in0=ln_sb[:, lo + 2:lo + 4],
            scalar1=1.0 / 1024.0, scalar2=None, op0=ALU.mult),
            inc=("V", 1), waits=(ev_su[0], ev_su[1]))
        ev_v1 = P.op("vector", lambda lo=lo: vc.scalar_tensor_tensor(
            out=ln_sb[:, lo + 6:lo + 8], in0=ln_sb[:, lo + 4:lo + 6], scalar=-1.0,
            in1=ln_sb[:, lo + 4:lo + 6], op0=ALU.mult, op1=ALU.mult),
            inc=("V", 1), waits=(ev_mn,))
        ev_var = P.op("vector", lambda lo=lo: vc.scalar_tensor_tensor(
            out=ln_sb[:, lo + 6:lo + 8], in0=ln_sb[:, lo:lo + 2],
            scalar=1.0 / 1024.0, in1=ln_sb[:, lo + 6:lo + 8],
            op0=ALU.mult, op1=ALU.add),
            inc=("V", 1), waits=(ev_v1, ev_sq[0], ev_sq[1]))
        ev_ln1 = P.op("scalar", lambda lo=lo: sc_e.activation(
            out=ln_sb[:, lo + 8:lo + 10], in_=ln_sb[:, lo + 6:lo + 8], func=AF.Ln,
            bias=cst[:, 0:1], scale=1.0), inc=("S", 1), waits=(ev_var, ev_consts))
        ev_rs = P.op("scalar", lambda lo=lo: sc_e.activation(
            out=ln_sb[:, lo + 8:lo + 10], in_=ln_sb[:, lo + 8:lo + 10],
            func=AF.Exp, scale=-0.5), inc=("S", 1), waits=(ev_ln1,))
        ev_nm1 = P.op("vector", lambda lo=lo: vc.tensor_mul(
            ln_sb[:, lo + 10:lo + 12], ln_sb[:, lo + 4:lo + 6],
            ln_sb[:, lo + 8:lo + 10]), inc=("V", 1), waits=(ev_rs,))
        ev_nm = P.op("vector", lambda lo=lo: vc.tensor_scalar(
            out=ln_sb[:, lo + 10:lo + 12], in0=ln_sb[:, lo + 10:lo + 12],
            scalar1=-1.0, scalar2=None, op0=ALU.mult),
            inc=("V", 1), waits=(ev_nm1,))
        for s in range(2):
            sl = slice(s * 1024, (s + 1) * 1024)
            P.op("scalar", lambda sl=sl, s=s, bi=bi, lo=lo: sc_e.activation(
                out=xn_tm2[bi][:, sl], in_=x2[bi][:, sl], func=AF.Identity,
                bias=ln_sb[:, lo + 10 + s:lo + 11 + s],
                scale=ln_sb[:, lo + 8 + s:lo + 9 + s]),
                inc=("S", 1), waits=(ev_nm, rdy(("xn_used", t - 2))),
                key=("xn", t, s))
            ev_xreads.append(P.r[("xn", t, s)])
        P.r[("xfree", t)] = tuple(ev_xreads)
        P.r[("xAev", t)] = ev_x

    emit_ln(0)
    rot.banks = [0, 1, 2, 3, 4, 5]
    rot.i = 0
    for t in range(a_tiles):
        bi = t % 2
        if t + 1 < a_tiles:
            emit_ln(t + 1)
        ev_x = rdy(("xAev", t))
        ev_tap_xn = tap("xn0", [128, 2048], BF16, xn_tm2[bi],
                        [rdy(("xn", t, 1))]) if t == TAP_T else None
        for s in range(2):
            j3, evf = rot.take()
            for j in range(8):
                w = (rdy(("xn", t, s)), evf, ev_wload) if j == 0 else ()
                P.op("tensor", lambda j3=j3, j=j, s=s, bi=bi: tn.transpose(
                    PMX_BF[j3][:, j * 128:(j + 1) * 128],
                    xn_tm2[bi][:, s * 1024 + j * 128: s * 1024 + (j + 1) * 128],
                    id_sb[:, :]),
                    inc=("T", 1) if j == 7 else None, waits=w,
                    key=("xnT", t, s) if j == 7 else None, note="xnT")
            ev_cp = P.op("vector", lambda s=s, j3=j3, bi=bi: vc.tensor_copy(
                out=xnfm2[bi].rearrange("p (j c) -> p j c", j=8)[:, :, s * 128:(s + 1) * 128],
                in_=PMX_BF[j3].rearrange("p (j c) -> p j c", j=8)),
                inc=("V", 1), waits=(rdy(("xnT", t, s)), rdy(("xnfm_used", t - 2))),
                key=("xnfm", t, s))
            rot.give(j3, ev_cp)
        ev_xnfm = (rdy(("xnfm", t, 0)), rdy(("xnfm", t, 1)))
        P.r[("xn_used", t)] = (rdy(("xnfm", t, 1)), ev_tap_xn)
        ev_spxn = P.op("gpsimd", lambda t=t, bi=bi: gp.dma_start(
            out=sp_xn[t], in_=xnfm2[bi][:, :]), inc=(f"SX{bi}", 16),
            waits=(ev_xnfm,))
        for kc in range(8):
            j3, evf = rot.take()
            bank = PMX[j3]
            jobs = [(bank[:, 0:256],
                     wk_sb[:, ic * 1024 + kc * 128: ic * 1024 + (kc + 1) * 128],
                     xnfm2[bi][:, ic * 256:(ic + 1) * 256],
                     ic == 0, ic == 7, None) for ic in range(8)]
            mm_seq(jobs, (ev_xnfm, evf), ("kmm", t, kc))
            ev_ev = P.op("scalar", lambda kc=kc, bank=bank, bi=bi: sc_e.activation(
                out=kq_fm2[bi][:, kc * 256:(kc + 1) * 256], in_=bank[:, 0:256],
                func=AF.Identity, bias=bk_sb[:, kc:kc + 1], scale=1.0),
                inc=("S", 1), waits=(rdy(("kmm", t, kc)), rdy(("kq_used", t - 2)),
                                     ev_consts),
                key=("kfm", t, kc))
            rot.give(j3, ev_ev)
        ev_kfm = rdy(("kfm", t, 7))
        ev_tap_kq = tap("kq0", [128, 2048], BF16, kq_fm2[bi],
                        [ev_kfm]) if t == TAP_T else None
        for hp in range(8):
            j3, evf = rot.take()
            bank = PMX[j3]
            jobs = [(bank[:, 0:256], pj_sb[:, hp * 128:(hp + 1) * 128],
                     kq_fm2[bi][:, hp * 256:(hp + 1) * 256], True, True, None)]
            mm_seq(jobs, (rdy(("kfm", t, hp)), evf), ("kpmm", t, hp))
            ev_ev = P.op("scalar", lambda hp=hp, bank=bank, bi=bi: sc_e.activation(
                out=kpfm2[bi][:, hp * 256:(hp + 1) * 256], in_=bank[:, 0:256],
                func=AF.Relu), inc=("S", 1),
                waits=(rdy(("kpmm", t, hp)), rdy(("kpfm_used", t - 2))),
                key=("kpfm", t, hp))
            rot.give(j3, ev_ev)
        ev_kpfm = tuple(rdy(("kpfm", t, i)) for i in range(8))
        ev_tap_kp = tap("kpfm0", [128, 2048], BF16, kpfm2[bi],
                        [ev_kpfm]) if t == TAP_T else None
        ev_spkp = P.op("gpsimd", lambda t=t, bi=bi: gp.dma_start(
            out=sp_kp[t], in_=kpfm2[bi][:, :]), inc=(f"SP{bi}", 16),
            waits=(ev_kpfm,))
        for tc in range(2):
            for half in range(2):
                j3, evf = rot.take()
                bank = PMX[j3]
                jobs = []
                for k4 in range(4):
                    kc = half * 4 + k4
                    jobs.append((bank[:, k4 * 128:(k4 + 1) * 128],
                                 kq_fm2[bi][:, kc * 256 + tc * 128: kc * 256 + tc * 128 + 128],
                                 pj_sb[:, kc * 128:(kc + 1) * 128],
                                 True, True, None))
                mm_seq(jobs, (ev_kfm, evf), ("ktmm", t, tc, half))
                ev_ev = P.op("scalar", lambda tc=tc, half=half, bank=bank, bi=bi:
                             sc_e.activation(
                                 out=ktm2[bi][:, tc * 1024 + half * 512:
                                              tc * 1024 + half * 512 + 512],
                                 in_=bank[:, :], func=AF.Relu), inc=("S", 1),
                             waits=(rdy(("ktmm", t, tc, half)),
                                    rdy(("ktm_used", t - 2))),
                             key=("ktm", t, tc, half))
                rot.give(j3, ev_ev)
        ev_ktm = tuple(rdy(("ktm", t, a, b)) for a in range(2) for b in range(2))
        ev_tap_kt = tap("ktm0", [128, 2048], BF16, ktm2[bi],
                        [ev_ktm]) if t == TAP_T else None
        ev_spkt = P.op("gpsimd", lambda t=t, bi=bi: gp.dma_start(
            out=sp_kt[t], in_=ktm2[bi][:, :]), inc=(f"ST{bi}", 16),
            waits=(ev_ktm,))
        for tc in range(2):
            for n in range(2):
                j3, evf = rot.take()
                bank = PMX[j3]
                jobs = [(bank[:, :],
                         xnfm2[bi][:, ic * 256 + tc * 128: ic * 256 + tc * 128 + 128],
                         wv_sb[:, ic * 1024 + n * 512: ic * 1024 + (n + 1) * 512],
                         ic == 0, ic == 7, None) for ic in range(8)]
                mm_seq(jobs, (ev_xnfm, evf), ("vmm", t, tc, n))
                ev_ev = P.op("vector", lambda tc=tc, n=n, bank=bank, bi=bi:
                             vc.tensor_tensor(
                                 out=vaug2[bi].rearrange(
                                     "p (tc h d) -> p tc h d", tc=2, h=16
                                 )[:, tc, n * 8:(n + 1) * 8, 0:64],
                                 in0=bank.rearrange("p (h d) -> p h d", h=8),
                                 in1=vb_sb.rearrange("p (h d) -> p h d", h=16)[:, n * 8:(n + 1) * 8, :],
                                 op=ALU.add), inc=("V", 1),
                             waits=(rdy(("vmm", t, tc, n)), rdy(("vaug_used", t - 2)),
                                    rdy(("vones", bi)), ev_consts),
                             key=("vaug", t, tc, n))
                rot.give(j3, ev_ev)
        ev_vaug = tuple(rdy(("vaug", t, a, b)) for a in range(2) for b in range(2))
        ev_spva = P.op("gpsimd", lambda t=t, bi=bi: gp.dma_start(
            out=sp_va[t], in_=vaug2[bi][:, :]), inc=(f"SV{bi}", 16),
            waits=(ev_vaug,))
        ev_tap_va = tap("vaug0", [128, 2080], BF16, vaug2[bi],
                        [ev_vaug]) if t == TAP_T else None
        P.r[("xnfm_used", t)] = (("T", P.cnt["T"]), ev_spxn)
        P.r[("kq_used", t)] = (("T", P.cnt["T"]), ev_tap_kq)
        for tc in range(2):
            jobs0, jobs1 = [], []
            for h in range(H):
                j = h // 2
                bank = pg[0] if j < 4 else pg[1]
                jl = jobs0 if j < 4 else jobs1
                jl.append((bank[64 * (h % 2):64 * (h % 2) + 64,
                                65 * (j % 4):65 * (j % 4) + 65],
                           ktm2[bi][:, tc * 1024 + j * 128 + (h % 2) * 64:
                                    tc * 1024 + j * 128 + (h % 2) * 64 + 64],
                           vaug2[bi][:, tc * 1040 + h * 65: tc * 1040 + h * 65 + 65],
                           True, True, (0, 64 * (h % 2))))
            for bank_i, jl in ((0, jobs0), (1, jobs1)):
                mm_seq(jl, (rdy(("ktm", t, tc, 0)), rdy(("ktm", t, tc, 1)),
                            rdy(("vaug", t, tc, 0)), rdy(("vaug", t, tc, 1)),
                            rdy(("pgfA", bank_i))), ("dsA", t, tc, bank_i))
            P.op("vector", lambda: vc.tensor_tensor(
                out=S_run[:, 0:260], in0=S_run[:, 0:260], in1=pg[0][:, 0:260],
                op=ALU.add), inc=("V", 1),
                waits=(rdy(("dsA", t, tc, 0)), rdy(("SrunA",))), key=("pgfA", 0))
            P.op("vector", lambda: vc.tensor_tensor(
                out=S_run[:, 260:520], in0=S_run[:, 260:520], in1=pg[1][:, 0:260],
                op=ALU.add), inc=("V", 1),
                waits=(rdy(("dsA", t, tc, 1)),), key=("pgfA", 1))
        P.r[("ktm_used", t)] = (("T", P.cnt["T"]), ev_spkt, ev_tap_kt)
        P.r[("vaug_used", t)] = (("T", P.cnt["T"]), ev_spva, ev_tap_va)
        P.r[("kpfm_used", t)] = (ev_spkp, ev_tap_kp)
        P.r[("spill", t)] = (ev_spxn, ev_spkp, ev_spkt, ev_spva)

    if not exchange:
        for i in range(tap_n[0]):
            v = P.cnt[f"TP{i}"]
            P.q["sync"].append(lambda i=i, v=v: sy.wait_ge(P.hdl[f"TP{i}"], v))
        with stack:
            with nc.Block() as block:
                block.sync(lambda e: [f() for f in P.q["sync"]])
                block.tensor(lambda e: [f() for f in P.q["tensor"]])
                block.vector(lambda e: [f() for f in P.q["vector"]])
                block.scalar(lambda e: [f() for f in P.q["scalar"]])
                block.gpsimd(lambda e: [f() for f in P.q["gpsimd"]])
        return nc

    for cb in range(3):
        P.r[("pcf", cb)] = rot.free.get(3 + cb)
    rot.banks = [0, 1, 2]
    rot.i = 0
    ev_sl0 = rdy(("pgfA", 0))
    ev_sl1 = rdy(("pgfA", 1))
    ev_phA = (("T", P.cnt["T"]), ("S", P.cnt["S"]), ("V", P.cnt["V"]))
    ev_wq = None
    for ic in range(8):
        ev_wq = P.op("gpsimd", lambda ic=ic: gp.dma_start(
            out=wq_sb[:, ic * 1024:(ic + 1) * 1024],
            in_=qkv_h[ic * 128:(ic + 1) * 128, 0:1024]), inc=("WL", 16),
            waits=(ev_phA,) if ic == 0 else ())
    P.r[("wq",)] = ev_wq
    ev_tap_sloc = tap("sloc", [128, 520], F32, S_run, [ev_sl0, ev_sl1])
    ev_ccw = P.op("gpsimd", lambda: gp.dma_start(out=cc_in[:, :], in_=S_run[:, :]),
                  inc=("CCS", 16), waits=(ev_sl0, ev_sl1))
    ev_cc = P.op("gpsimd", lambda: gp.collective_compute(
        "AllGather", ALU.bypass, replica_groups=[list(range(NCORES))],
        ins=[cc_in[:, :]], outs=[cc_out[:, :]]), inc=("CC", 1), waits=(ev_ccw,))
    ev_z = P.op("vector", lambda: vc.memset(S_run[:, :], 0.0), inc=("V", 1),
                waits=(ev_ccw, ev_tap_sloc))
    prev = ev_z
    for r in range(NCORES):
        ev_g = P.op("gpsimd", lambda r=r: gp.dma_start(
            out=Sg[:, :], in_=cc_out[r * 128:(r + 1) * 128, :]),
            inc=("SGS", 16), waits=(ev_cc, prev))
        prev = P.op("vector", lambda r=r: vc.scalar_tensor_tensor(
            out=S_run[:, :], in0=Sg[:, :], scalar=rm_sb[:, r:r + 1],
            in1=S_run[:, :], op0=ALU.mult, op1=ALU.add),
            inc=("V", 1), waits=(ev_g, prev))
    ev_sinit = prev
    ev_tap_sinit = tap("sinit", [128, 520], F32, S_run, [ev_sinit])
    ev_sinit = (ev_sinit, ev_tap_sinit)
    ev_sbf = P.op("vector", lambda: vc.tensor_copy(out=S_bf[:, :], in_=S_run[:, :]),
                  inc=("V", 1), waits=(ev_sinit,))
    P.r[("Sbf",)] = ev_sbf
    P.r[("pgf", 0)] = ev_sl0
    P.r[("pgf", 1)] = ev_sl1

    def emit_chunk_at(t, bi, tc):
        am = amt2[tc]

        def at_group(grp):
            j3, evf = rot.take()
            bank = pm[j3]
            jobs = []
            for hh in range(4):
                h = grp * 4 + hh
                hp, hf = h // 2, h % 2
                off = hp * 256 + tc * 128
                qz = qp_top if hf == 0 else qp_bot
                jobs.append((bank[:, hh * 128:(hh + 1) * 128],
                             kpfm2[bi][:, off:off + 128],
                             qz[:, off:off + 128], True, True, None))
            mm_seq(jobs, (rdy(("qpok", t)), rdy(("lkp", t)), evf),
                   ("atmm", t, tc, grp))
            ev_ev = P.op("vector", lambda grp=grp, bank=bank, am=am: vc.tensor_tensor(
                out=am[:, grp * 512:(grp + 1) * 512], in0=bank[:, :],
                in1=mk_sb[:, :], op=ALU.mult), inc=("V", 1),
                waits=(rdy(("atmm", t, tc, grp)), rdy(("amt_used", tc, grp)),
                       ev_consts),
                key=("amt", t, tc, grp))
            rot.give(j3, ev_ev)

        def ctx_group(cb, amt_grps):
            jobs = []
            for h in range(16):
                if CTX_BANK[h] != cb:
                    continue
                hp, hf = h // 2, h % 2
                off = hp * 256 + tc * 128
                qz = qp_top if hf == 0 else qp_bot
                sco = 260 * (hp // 4) + 65 * (hp % 4)
                jobs.append((pc[cb][:, CTX_COL[h]:CTX_COL[h] + 65],
                             am[:, h * 128:(h + 1) * 128],
                             vaug2[bi][:, tc * 1040 + h * 65: tc * 1040 + h * 65 + 65],
                             True, False, None))
                jobs.append((pc[cb][:, CTX_COL[h]:CTX_COL[h] + 65],
                             qz[:, off:off + 128],
                             S_bf[:, sco:sco + 65], False, True, None))
            mm_seq(jobs, tuple(rdy(("amt", t, tc, g)) for g in amt_grps)
                   + (rdy(("lva", t)), rdy(("Sbf",)), rdy(("pcf", cb))),
                   ("ctx", t, tc, cb))

        at_group(0)
        at_group(1)
        ctx_group(0, (0, 1))
        at_group(2)
        ctx_group(1, (1, 2))
        at_group(3)
        ctx_group(2, (3,))

    def emit_chunk_mid(t, bi, tc):
        ev_ctx_t = ("T", P.cnt["T"])
        for g in range(4):
            P.r[("amt_used", tc, g)] = ev_ctx_t
        for bank_i in range(2):
            jobs = []
            for j4 in range(4):
                j = bank_i * 4 + j4
                for hf in range(2):
                    h = 2 * j + hf
                    jobs.append((pg[bank_i][64 * hf:64 * hf + 64,
                                            65 * j4:65 * j4 + 65],
                                 ktm2[bi][:, tc * 1024 + j * 128 + hf * 64:
                                          tc * 1024 + j * 128 + hf * 64 + 64],
                                 vaug2[bi][:, tc * 1040 + h * 65:
                                           tc * 1040 + h * 65 + 65],
                                 True, True, (0, 64 * hf)))
            mm_seq(jobs, (rdy(("lkt", t)), rdy(("lva", t)), rdy(("pgf", bank_i))),
                   ("dsB", t, tc, bank_i))
        nco = 32 * tc
        ev_nc = []
        for cb in range(3):
            nsl = [6, 6, 4][cb]
            no = [0, 6, 12][cb]
            ev_nc.append(P.op("vector", lambda cb=cb, nsl=nsl, no=no, nco=nco:
                              vc.tensor_copy(
                                  out=nrm_sb[:, nco + no: nco + no + nsl],
                                  in_=pc[cb][:, 0:65 * nsl].rearrange(
                                      "p (h d) -> p h d", d=65)[:, :, 64]),
                              inc=("V", 1), waits=(rdy(("ctx", t, tc, cb)),)))
        ev_na = P.op("vector", lambda nco=nco: vc.tensor_scalar(
            out=nrm_sb[:, nco + 16:nco + 32], in0=nrm_sb[:, nco:nco + 16],
            scalar1=1e-6, scalar2=None, op0=ALU.add),
            inc=("V", 1), waits=tuple(ev_nc))
        ev_rn = P.op("vector", lambda nco=nco: vc.reciprocal(
            out=nrm_sb[:, nco + 16:nco + 32], in_=nrm_sb[:, nco + 16:nco + 32]),
            inc=("V", 1), waits=(ev_na,))
        for cb in range(3):
            nsl = [6, 6, 4][cb]
            no = [0, 6, 12][cb]
            ev_ev = P.op("vector", lambda cb=cb, tc=tc, nsl=nsl, no=no, nco=nco:
                         vc.tensor_tensor(
                             out=ctxn[:, tc * 1024 + no * 64:
                                      tc * 1024 + (no + nsl) * 64]
                             .rearrange("p (h d) -> p h d", d=64),
                             in0=pc[cb][:, 0:65 * nsl].rearrange(
                                 "p (h d) -> p h d", d=65)[:, :, 0:64],
                             in1=nrm_sb[:, nco + 16 + no: nco + 16 + no + nsl][:, :, None]
                             .broadcast_to([128, nsl, 64]),
                             op=ALU.mult), inc=("V", 1),
                         waits=(ev_rn, rdy(("ctxn_used", tc))),
                         key=("ctxn", t, tc, cb))
            P.r[("pcf", cb)] = ev_ev
        ev_a0 = P.op("vector", lambda: vc.tensor_tensor(
            out=S_run[:, 0:260], in0=S_run[:, 0:260], in1=pg[0][:, 0:260],
            op=ALU.add), inc=("V", 1),
            waits=(rdy(("dsB", t, tc, 0)), ev_sinit))
        ev_a1 = P.op("vector", lambda: vc.tensor_tensor(
            out=S_run[:, 260:520], in0=S_run[:, 260:520], in1=pg[1][:, 0:260],
            op=ALU.add), inc=("V", 1), waits=(rdy(("dsB", t, tc, 1)),))
        P.r[("pgf", 0)] = ev_a0
        P.r[("pgf", 1)] = ev_a1
        ev_sbf2 = P.op("vector", lambda: vc.tensor_copy(
            out=S_bf[:, :], in_=S_run[:, :]), inc=("V", 1),
            waits=(ev_a0, ev_a1, ev_ctx_t))
        P.r[("Sbf",)] = ev_sbf2

    def emit_chunk_back(t, tc):
        ev_ctxn = tuple(rdy(("ctxn", t, tc, cb)) for cb in range(3))
        j3, evf = rot.take()
        for i in range(8):
            w = ev_ctxn + (evf,) if i == 0 else ()
            P.op("tensor", lambda i=i, tc=tc, j3=j3: tn.transpose(
                pm_bf[j3][:, i * 128:(i + 1) * 128],
                ctxn[:, tc * 1024 + i * 128: tc * 1024 + (i + 1) * 128],
                id_sb[:, :]),
                inc=("T", 1) if i == 7 else None, waits=w,
                key=("ctxT", t, tc) if i == 7 else None, note=f"ctxT{tc}")
        ev_cf = P.op("vector", lambda tc=tc, j3=j3: vc.tensor_copy(
            out=ctxfm.rearrange("p (j c) -> p j c", j=8)[:, :, tc * 128:(tc + 1) * 128],
            in_=pm_bf[j3].rearrange("p (j c) -> p j c", j=8)),
            inc=("V", 1), waits=(rdy(("ctxT", t, tc)), rdy(("ctxfm_used", t - 1))),
            key=("ctxfm", t, tc))
        rot.give(j3, ev_cf)
        P.r[("ctxn_used", tc)] = rdy(("ctxT", t, tc))

    def emit_flow(key, lhs_list, rhs_list, wait_extra, evac):
        j3, evf = rot.take()
        bank = pm[j3]
        jobs = [(bank[:, :], lhs_list[ic], rhs_list[ic], ic == 0, ic == 7, None)
                for ic in range(8)]
        mm_seq(jobs, tuple(wait_extra) + (evf,), key)
        ev = evac(bank)
        rot.give(j3, ev)
        return ev

    def emit_epilogue(t):
        bi = t % 2
        for tc in range(2):
            for n in range(2):
                def evac_s(bank, t=t, tc=tc, n=n):
                    return P.op("vector", lambda tc=tc, n=n, bank=bank:
                                vc.tensor_tensor(
                                    out=s_sb[:, tc * 1024 + n * 512:
                                             tc * 1024 + n * 512 + 512],
                                    in0=bank[:, :],
                                    in1=ob_sb[:, n * 512:(n + 1) * 512],
                                    op=ALU.add), inc=("V", 1),
                                waits=(rdy(("smm", t, tc, n)),
                                       rdy(("s_out", t - 1))),
                                key=("s", t, tc, n))

                emit_flow(("smm", t, tc, n),
                          [ctxfm[:, ic * 256 + tc * 128: ic * 256 + tc * 128 + 128]
                           for ic in range(8)],
                          [wo_sb[:, ic * 1024 + n * 512: ic * 1024 + (n + 1) * 512]
                           for ic in range(8)],
                          (rdy(("ctxfm", t, tc)),), evac_s)
        P.r[("ctxfm_used", t)] = ("T", P.cnt["T"])
        ev_tap_s = tap("s0", [128, 2048], F32, s_sb,
                       [rdy(("s", 0, tcc, n)) for tcc in range(2)
                        for n in range(2)]) if t == 0 else None
        ev_fin = []
        for tc in range(2):
            for n in range(2):
                sl = slice(tc * 1024 + n * 512, tc * 1024 + n * 512 + 512)
                ev_m = P.op("gpsimd", lambda sl=sl: gp.tensor_tensor(
                    out=g_sb[:, sl], in0=g_sb[:, sl], in1=s_sb[:, sl],
                    op=ALU.mult), inc=("P", 1),
                    waits=(rdy(("g", t, tc, n)), rdy(("s", t, tc, n))))
                ev_f = P.op("gpsimd", lambda sl=sl, bi=bi: gp.tensor_tensor(
                    out=s_sb[:, sl], in0=g_sb[:, sl], in1=x2[bi][:, sl],
                    op=ALU.add), inc=("P", 1), waits=(ev_m, rdy(("xB", t))))
                ev_fin.append(ev_f)
        P.r[("g_out", t)] = tuple(ev_fin)
        ev_tap_sg = tap("sg0", [128, 2048], F32, s_sb,
                        list(ev_fin)) if t == 0 else None
        ev_st = P.op("gpsimd", lambda t=t: gp.dma_start(
            out=y_h[t * TT:(t + 1) * TT, :].rearrange("(s p) e -> p s e", p=128),
            in_=s_sb.rearrange("p (s e) -> p s e", s=2)),
            inc=("YO", 16), waits=tuple(ev_fin), key=("yout", t))
        P.r[("s_out", t)] = (ev_st, ev_tap_s, ev_tap_sg)
        P.r[("xBfree", t)] = tuple(ev_fin)

    for t in range(b_tiles):
        bi = t % 2
        ev_x = P.op("sync", lambda t=t, bi=bi: sy.dma_start(
            out=x2[bi].rearrange("p (s e) -> p s e", s=2),
            in_=x_h[t * TT:(t + 1) * TT, :].rearrange("(s p) e -> p s e", p=128)),
            inc=(f"LX{bi}", 16),
            waits=(rdy(("xBfree", t - 2)), rdy(("xfree", a_tiles - 1))),
            key=("xB", t))
        ev_lxn = P.op("sync", lambda t=t, bi=bi: sy.dma_start(
            out=xnfm2[bi][:, :], in_=sp_xn[t]), inc=(f"LN{bi}", 16),
            waits=(rdy(("xnfmB_used", t - 2)), rdy(("xnfm_used", a_tiles - 1)),
                   rdy(("spill", t))))
        ev_lkp = P.op("sync", lambda t=t, bi=bi: sy.dma_start(
            out=kpfm2[bi][:, :], in_=sp_kp[t]), inc=(f"LP{bi}", 16),
            waits=(rdy(("kpfmB_used", t - 2)), rdy(("kpfm_used", a_tiles - 1)),
                   rdy(("spill", t))), key=("lkp", t))
        ev_lkt = P.op("sync", lambda t=t, bi=bi: sy.dma_start(
            out=ktm2[bi][:, :], in_=sp_kt[t]), inc=(f"LT{bi}", 16),
            waits=(rdy(("ktmB_used", t - 2)), rdy(("ktm_used", a_tiles - 1)),
                   rdy(("spill", t))), key=("lkt", t))
        ev_lva = P.op("sync", lambda t=t, bi=bi: sy.dma_start(
            out=vaug2[bi][:, :], in_=sp_va[t]), inc=(f"LV{bi}", 16),
            waits=(rdy(("vaugB_used", t - 2)), rdy(("vaug_used", a_tiles - 1)),
                   rdy(("spill", t))), key=("lva", t))
        if t > 0:
            emit_chunk_back(t - 1, 1)
            emit_epilogue(t - 1)
        for qc in (0,):
            j3, evf = rot.take()
            bank = pm[j3]
            jobs = [(bank[:, 0:256],
                     wq_sb[:, ic * 1024 + qc * 128: ic * 1024 + (qc + 1) * 128],
                     xnfm2[bi][:, ic * 256:(ic + 1) * 256],
                     ic == 0, ic == 7, None) for ic in range(8)]
            mm_seq(jobs, (ev_lxn, evf, rdy(("wq",))), ("qmm", t, qc))
            ev_ev = P.op("scalar", lambda qc=qc, bank=bank: sc_e.activation(
                out=kq_fm0[:, qc * 256:(qc + 1) * 256], in_=bank[:, 0:256],
                func=AF.Identity, bias=bq_sb[:, qc:qc + 1], scale=1.0),
                inc=("S", 1), waits=(rdy(("qmm", t, qc)), rdy(("kqB_used", t - 1))),
                key=("qfm", t, qc))
            rot.give(j3, ev_ev)
        for qc in (1,):
            j3, evf = rot.take()
            bank = pm[j3]
            jobs = [(bank[:, 0:256],
                     wq_sb[:, ic * 1024 + qc * 128: ic * 1024 + (qc + 1) * 128],
                     xnfm2[bi][:, ic * 256:(ic + 1) * 256],
                     ic == 0, ic == 7, None) for ic in range(8)]
            mm_seq(jobs, (ev_lxn, evf, rdy(("wq",))), ("qmm", t, qc))
            ev_ev = P.op("scalar", lambda qc=qc, bank=bank: sc_e.activation(
                out=kq_fm0[:, qc * 256:(qc + 1) * 256], in_=bank[:, 0:256],
                func=AF.Identity, bias=bq_sb[:, qc:qc + 1], scale=1.0),
                inc=("S", 1), waits=(rdy(("qmm", t, qc)), rdy(("kqB_used", t - 1))),
                key=("qfm", t, qc))
            rot.give(j3, ev_ev)
        for qc in (2,):
            j3, evf = rot.take()
            bank = pm[j3]
            jobs = [(bank[:, 0:256],
                     wq_sb[:, ic * 1024 + qc * 128: ic * 1024 + (qc + 1) * 128],
                     xnfm2[bi][:, ic * 256:(ic + 1) * 256],
                     ic == 0, ic == 7, None) for ic in range(8)]
            mm_seq(jobs, (ev_lxn, evf, rdy(("wq",))), ("qmm", t, qc))
            ev_ev = P.op("scalar", lambda qc=qc, bank=bank: sc_e.activation(
                out=kq_fm0[:, qc * 256:(qc + 1) * 256], in_=bank[:, 0:256],
                func=AF.Identity, bias=bq_sb[:, qc:qc + 1], scale=1.0),
                inc=("S", 1), waits=(rdy(("qmm", t, qc)), rdy(("kqB_used", t - 1))),
                key=("qfm", t, qc))
            rot.give(j3, ev_ev)
        for hp in (0,):
            j3, evf = rot.take()
            bank = pm[j3]
            jobs = [(bank[:, 0:256], pj_sb[:, hp * 128:(hp + 1) * 128],
                     kq_fm0[:, hp * 256:(hp + 1) * 256], True, True, None)]
            mm_seq(jobs, (rdy(("qfm", t, hp)), evf), ("qpmm", t, hp))
            P.op("scalar", lambda hp=hp, bank=bank: sc_e.activation(
                out=qp_top[0:64, hp * 256:(hp + 1) * 256], in_=bank[0:64, 0:256],
                func=AF.Relu), inc=("S", 1),
                waits=(rdy(("qpmm", t, hp)), rdy(("qp_used", t - 1)),
                       rdy(("qpz", 0)), rdy(("qpz", 1))))
            ev_ev = P.op("scalar", lambda hp=hp, bank=bank: sc_e.activation(
                out=qp_bot[64:128, hp * 256:(hp + 1) * 256], in_=bank[64:128, 0:256],
                func=AF.Relu), inc=("S", 1), key=("qp", t, hp))
            rot.give(j3, ev_ev)
        for qc in (3,):
            j3, evf = rot.take()
            bank = pm[j3]
            jobs = [(bank[:, 0:256],
                     wq_sb[:, ic * 1024 + qc * 128: ic * 1024 + (qc + 1) * 128],
                     xnfm2[bi][:, ic * 256:(ic + 1) * 256],
                     ic == 0, ic == 7, None) for ic in range(8)]
            mm_seq(jobs, (ev_lxn, evf, rdy(("wq",))), ("qmm", t, qc))
            ev_ev = P.op("scalar", lambda qc=qc, bank=bank: sc_e.activation(
                out=kq_fm0[:, qc * 256:(qc + 1) * 256], in_=bank[:, 0:256],
                func=AF.Identity, bias=bq_sb[:, qc:qc + 1], scale=1.0),
                inc=("S", 1), waits=(rdy(("qmm", t, qc)), rdy(("kqB_used", t - 1))),
                key=("qfm", t, qc))
            rot.give(j3, ev_ev)
        for hp in (1,):
            j3, evf = rot.take()
            bank = pm[j3]
            jobs = [(bank[:, 0:256], pj_sb[:, hp * 128:(hp + 1) * 128],
                     kq_fm0[:, hp * 256:(hp + 1) * 256], True, True, None)]
            mm_seq(jobs, (rdy(("qfm", t, hp)), evf), ("qpmm", t, hp))
            P.op("scalar", lambda hp=hp, bank=bank: sc_e.activation(
                out=qp_top[0:64, hp * 256:(hp + 1) * 256], in_=bank[0:64, 0:256],
                func=AF.Relu), inc=("S", 1),
                waits=(rdy(("qpmm", t, hp)), rdy(("qp_used", t - 1)),
                       rdy(("qpz", 0)), rdy(("qpz", 1))))
            ev_ev = P.op("scalar", lambda hp=hp, bank=bank: sc_e.activation(
                out=qp_bot[64:128, hp * 256:(hp + 1) * 256], in_=bank[64:128, 0:256],
                func=AF.Relu), inc=("S", 1), key=("qp", t, hp))
            rot.give(j3, ev_ev)
        for qc in (4,):
            j3, evf = rot.take()
            bank = pm[j3]
            jobs = [(bank[:, 0:256],
                     wq_sb[:, ic * 1024 + qc * 128: ic * 1024 + (qc + 1) * 128],
                     xnfm2[bi][:, ic * 256:(ic + 1) * 256],
                     ic == 0, ic == 7, None) for ic in range(8)]
            mm_seq(jobs, (ev_lxn, evf, rdy(("wq",))), ("qmm", t, qc))
            ev_ev = P.op("scalar", lambda qc=qc, bank=bank: sc_e.activation(
                out=kq_fm0[:, qc * 256:(qc + 1) * 256], in_=bank[:, 0:256],
                func=AF.Identity, bias=bq_sb[:, qc:qc + 1], scale=1.0),
                inc=("S", 1), waits=(rdy(("qmm", t, qc)), rdy(("kqB_used", t - 1))),
                key=("qfm", t, qc))
            rot.give(j3, ev_ev)
        for hp in (2,):
            j3, evf = rot.take()
            bank = pm[j3]
            jobs = [(bank[:, 0:256], pj_sb[:, hp * 128:(hp + 1) * 128],
                     kq_fm0[:, hp * 256:(hp + 1) * 256], True, True, None)]
            mm_seq(jobs, (rdy(("qfm", t, hp)), evf), ("qpmm", t, hp))
            P.op("scalar", lambda hp=hp, bank=bank: sc_e.activation(
                out=qp_top[0:64, hp * 256:(hp + 1) * 256], in_=bank[0:64, 0:256],
                func=AF.Relu), inc=("S", 1),
                waits=(rdy(("qpmm", t, hp)), rdy(("qp_used", t - 1)),
                       rdy(("qpz", 0)), rdy(("qpz", 1))))
            ev_ev = P.op("scalar", lambda hp=hp, bank=bank: sc_e.activation(
                out=qp_bot[64:128, hp * 256:(hp + 1) * 256], in_=bank[64:128, 0:256],
                func=AF.Relu), inc=("S", 1), key=("qp", t, hp))
            rot.give(j3, ev_ev)
        for qc in (5,):
            j3, evf = rot.take()
            bank = pm[j3]
            jobs = [(bank[:, 0:256],
                     wq_sb[:, ic * 1024 + qc * 128: ic * 1024 + (qc + 1) * 128],
                     xnfm2[bi][:, ic * 256:(ic + 1) * 256],
                     ic == 0, ic == 7, None) for ic in range(8)]
            mm_seq(jobs, (ev_lxn, evf, rdy(("wq",))), ("qmm", t, qc))
            ev_ev = P.op("scalar", lambda qc=qc, bank=bank: sc_e.activation(
                out=kq_fm0[:, qc * 256:(qc + 1) * 256], in_=bank[:, 0:256],
                func=AF.Identity, bias=bq_sb[:, qc:qc + 1], scale=1.0),
                inc=("S", 1), waits=(rdy(("qmm", t, qc)), rdy(("kqB_used", t - 1))),
                key=("qfm", t, qc))
            rot.give(j3, ev_ev)
        for hp in (3,):
            j3, evf = rot.take()
            bank = pm[j3]
            jobs = [(bank[:, 0:256], pj_sb[:, hp * 128:(hp + 1) * 128],
                     kq_fm0[:, hp * 256:(hp + 1) * 256], True, True, None)]
            mm_seq(jobs, (rdy(("qfm", t, hp)), evf), ("qpmm", t, hp))
            P.op("scalar", lambda hp=hp, bank=bank: sc_e.activation(
                out=qp_top[0:64, hp * 256:(hp + 1) * 256], in_=bank[0:64, 0:256],
                func=AF.Relu), inc=("S", 1),
                waits=(rdy(("qpmm", t, hp)), rdy(("qp_used", t - 1)),
                       rdy(("qpz", 0)), rdy(("qpz", 1))))
            ev_ev = P.op("scalar", lambda hp=hp, bank=bank: sc_e.activation(
                out=qp_bot[64:128, hp * 256:(hp + 1) * 256], in_=bank[64:128, 0:256],
                func=AF.Relu), inc=("S", 1), key=("qp", t, hp))
            rot.give(j3, ev_ev)
        for qc in (6,):
            j3, evf = rot.take()
            bank = pm[j3]
            jobs = [(bank[:, 0:256],
                     wq_sb[:, ic * 1024 + qc * 128: ic * 1024 + (qc + 1) * 128],
                     xnfm2[bi][:, ic * 256:(ic + 1) * 256],
                     ic == 0, ic == 7, None) for ic in range(8)]
            mm_seq(jobs, (ev_lxn, evf, rdy(("wq",))), ("qmm", t, qc))
            ev_ev = P.op("scalar", lambda qc=qc, bank=bank: sc_e.activation(
                out=kq_fm0[:, qc * 256:(qc + 1) * 256], in_=bank[:, 0:256],
                func=AF.Identity, bias=bq_sb[:, qc:qc + 1], scale=1.0),
                inc=("S", 1), waits=(rdy(("qmm", t, qc)), rdy(("kqB_used", t - 1))),
                key=("qfm", t, qc))
            rot.give(j3, ev_ev)
        for hp in (4,):
            j3, evf = rot.take()
            bank = pm[j3]
            jobs = [(bank[:, 0:256], pj_sb[:, hp * 128:(hp + 1) * 128],
                     kq_fm0[:, hp * 256:(hp + 1) * 256], True, True, None)]
            mm_seq(jobs, (rdy(("qfm", t, hp)), evf), ("qpmm", t, hp))
            P.op("scalar", lambda hp=hp, bank=bank: sc_e.activation(
                out=qp_top[0:64, hp * 256:(hp + 1) * 256], in_=bank[0:64, 0:256],
                func=AF.Relu), inc=("S", 1),
                waits=(rdy(("qpmm", t, hp)), rdy(("qp_used", t - 1)),
                       rdy(("qpz", 0)), rdy(("qpz", 1))))
            ev_ev = P.op("scalar", lambda hp=hp, bank=bank: sc_e.activation(
                out=qp_bot[64:128, hp * 256:(hp + 1) * 256], in_=bank[64:128, 0:256],
                func=AF.Relu), inc=("S", 1), key=("qp", t, hp))
            rot.give(j3, ev_ev)
        for qc in (7,):
            j3, evf = rot.take()
            bank = pm[j3]
            jobs = [(bank[:, 0:256],
                     wq_sb[:, ic * 1024 + qc * 128: ic * 1024 + (qc + 1) * 128],
                     xnfm2[bi][:, ic * 256:(ic + 1) * 256],
                     ic == 0, ic == 7, None) for ic in range(8)]
            mm_seq(jobs, (ev_lxn, evf, rdy(("wq",))), ("qmm", t, qc))
            ev_ev = P.op("scalar", lambda qc=qc, bank=bank: sc_e.activation(
                out=kq_fm0[:, qc * 256:(qc + 1) * 256], in_=bank[:, 0:256],
                func=AF.Identity, bias=bq_sb[:, qc:qc + 1], scale=1.0),
                inc=("S", 1), waits=(rdy(("qmm", t, qc)), rdy(("kqB_used", t - 1))),
                key=("qfm", t, qc))
            rot.give(j3, ev_ev)
        for hp in (5,):
            j3, evf = rot.take()
            bank = pm[j3]
            jobs = [(bank[:, 0:256], pj_sb[:, hp * 128:(hp + 1) * 128],
                     kq_fm0[:, hp * 256:(hp + 1) * 256], True, True, None)]
            mm_seq(jobs, (rdy(("qfm", t, hp)), evf), ("qpmm", t, hp))
            P.op("scalar", lambda hp=hp, bank=bank: sc_e.activation(
                out=qp_top[0:64, hp * 256:(hp + 1) * 256], in_=bank[0:64, 0:256],
                func=AF.Relu), inc=("S", 1),
                waits=(rdy(("qpmm", t, hp)), rdy(("qp_used", t - 1)),
                       rdy(("qpz", 0)), rdy(("qpz", 1))))
            ev_ev = P.op("scalar", lambda hp=hp, bank=bank: sc_e.activation(
                out=qp_bot[64:128, hp * 256:(hp + 1) * 256], in_=bank[64:128, 0:256],
                func=AF.Relu), inc=("S", 1), key=("qp", t, hp))
            rot.give(j3, ev_ev)
        for hp in (6,):
            j3, evf = rot.take()
            bank = pm[j3]
            jobs = [(bank[:, 0:256], pj_sb[:, hp * 128:(hp + 1) * 128],
                     kq_fm0[:, hp * 256:(hp + 1) * 256], True, True, None)]
            mm_seq(jobs, (rdy(("qfm", t, hp)), evf), ("qpmm", t, hp))
            P.op("scalar", lambda hp=hp, bank=bank: sc_e.activation(
                out=qp_top[0:64, hp * 256:(hp + 1) * 256], in_=bank[0:64, 0:256],
                func=AF.Relu), inc=("S", 1),
                waits=(rdy(("qpmm", t, hp)), rdy(("qp_used", t - 1)),
                       rdy(("qpz", 0)), rdy(("qpz", 1))))
            ev_ev = P.op("scalar", lambda hp=hp, bank=bank: sc_e.activation(
                out=qp_bot[64:128, hp * 256:(hp + 1) * 256], in_=bank[64:128, 0:256],
                func=AF.Relu), inc=("S", 1), key=("qp", t, hp))
            rot.give(j3, ev_ev)
        for hp in (7,):
            j3, evf = rot.take()
            bank = pm[j3]
            jobs = [(bank[:, 0:256], pj_sb[:, hp * 128:(hp + 1) * 128],
                     kq_fm0[:, hp * 256:(hp + 1) * 256], True, True, None)]
            mm_seq(jobs, (rdy(("qfm", t, hp)), evf), ("qpmm", t, hp))
            P.op("scalar", lambda hp=hp, bank=bank: sc_e.activation(
                out=qp_top[0:64, hp * 256:(hp + 1) * 256], in_=bank[0:64, 0:256],
                func=AF.Relu), inc=("S", 1),
                waits=(rdy(("qpmm", t, hp)), rdy(("qp_used", t - 1)),
                       rdy(("qpz", 0)), rdy(("qpz", 1))))
            ev_ev = P.op("scalar", lambda hp=hp, bank=bank: sc_e.activation(
                out=qp_bot[64:128, hp * 256:(hp + 1) * 256], in_=bank[64:128, 0:256],
                func=AF.Relu), inc=("S", 1), key=("qp", t, hp))
            rot.give(j3, ev_ev)
        P.r[("qpok", t)] = tuple(rdy(("qp", t, hp)) for hp in range(8))
        P.r[("kqB_used", t)] = ("T", P.cnt["T"])
        emit_chunk_at(t, bi, 0)
        emit_chunk_mid(t, bi, 0)
        for tc in range(2):
            for n in range(2):
                sl = slice(tc * 1024 + n * 512, tc * 1024 + n * 512 + 512)

                def evac_ffn(bank, sl=sl, n=n, t=t, tc=tc):
                    ev_e1 = P.op("vector", lambda sl=sl, n=n, bank=bank:
                                 vc.tensor_tensor(
                                     out=g_sb[:, sl], in0=bank[:, :],
                                     in1=sc_sb[:, n * 512:(n + 1) * 512],
                                     op=ALU.mult), inc=("V", 1),
                                 waits=(rdy(("fmm", t, tc, n)),
                                        rdy(("g_out", t - 1))))
                    ev_e2 = P.op("vector", lambda sl=sl, n=n: vc.tensor_tensor(
                        out=g_sb[:, sl], in0=g_sb[:, sl],
                        in1=fb_sb[:, n * 512:(n + 1) * 512],
                        op=ALU.add), inc=("V", 1), waits=(ev_e1,))
                    ev_e3 = P.op("vector", lambda sl=sl: vc.tensor_scalar(
                        out=g_sb[:, sl], in0=g_sb[:, sl], scalar1=100.0,
                        scalar2=-100.0, op0=ALU.min, op1=ALU.max),
                        inc=("V", 1), waits=(ev_e2,))
                    P.op("scalar", lambda sl=sl: sc_e.activation(
                        out=g_sb[:, sl], in_=g_sb[:, sl], func=AF.Silu),
                        inc=("S", 1), waits=(ev_e3,), key=("g", t, tc, n))
                    return ev_e1

                emit_flow(("fmm", t, tc, n),
                          [xnfm2[bi][:, ic * 256 + tc * 128:
                                     ic * 256 + tc * 128 + 128] for ic in range(8)],
                          [ws_sb[:, ic * 1024 + n * 512: ic * 1024 + (n + 1) * 512]
                           for ic in range(8)],
                          (ev_lxn, ev_sgn), evac_ffn)
        emit_chunk_at(t, bi, 1)
        emit_chunk_back(t, 0)
        emit_chunk_mid(t, bi, 1)
        P.r[("qp_used", t)] = ("T", P.cnt["T"])
        P.r[("kpfmB_used", t)] = ("T", P.cnt["T"])
        P.r[("ktmB_used", t)] = ("T", P.cnt["T"])
        P.r[("xnfmB_used", t)] = ("T", P.cnt["T"])
        P.r[("vaugB_used", t)] = ("T", P.cnt["T"])
        if t == 0:
            ev_tp = tap("ctxn0", [128, 2048], BF16, ctxn,
                        [rdy(("ctxn", 0, tcc, cb)) for tcc in range(2)
                         for cb in range(3)])
            if ev_tp:
                P.r[("ctxn_used", 1)] = (P.r.get(("ctxn_used", 1)), ev_tp)

    if b_tiles > 0:
        emit_chunk_back(b_tiles - 1, 1)
        emit_epilogue(b_tiles - 1)

    fq = P.cnt["YO"]
    if fq:
        P.q["sync"].append(lambda v=fq: sy.wait_ge(P.hdl["YO"], v))
    for i in range(tap_n[0]):
        v = P.cnt[f"TP{i}"]
        P.q["sync"].append(lambda i=i, v=v: sy.wait_ge(P.hdl[f"TP{i}"], v))

    with stack:
        with nc.Block() as block:
            block.sync(lambda e: [f() for f in P.q["sync"]])
            block.tensor(lambda e: [f() for f in P.q["tensor"]])
            block.vector(lambda e: [f() for f in P.q["vector"]])
            block.scalar(lambda e: [f() for f in P.q["scalar"]])
            block.gpsimd(lambda e: [f() for f in P.q["gpsimd"]])
    return nc


def prep_in_maps(L0_attn_x, L0_attn_qkv_w, L0_attn_qkv_b, L0_attn_out_w,
                 L0_attn_out_b, L0_attn_proj, L0_ffn_weights, L0_ffn_scale,
                 L0_ffn_bias):
    x = np.ascontiguousarray(np.asarray(L0_attn_x, dtype=np.float32))
    qkv_w = np.ascontiguousarray(np.asarray(L0_attn_qkv_w, dtype=np.float32))
    qkv_b = np.asarray(L0_attn_qkv_b, dtype=np.float32)
    out_w = np.ascontiguousarray(np.asarray(L0_attn_out_w, dtype=np.float32))
    out_b = np.asarray(L0_attn_out_b, dtype=np.float32)
    proj = np.asarray(L0_attn_proj, dtype=np.float32)
    fw = np.asarray(L0_ffn_weights, dtype=np.float32)
    fscale = np.asarray(L0_ffn_scale, dtype=np.float32)
    fbias = np.asarray(L0_ffn_bias, dtype=np.float32)

    projbd = np.zeros((8, 128, 128), np.float32)
    for j in range(8):
        projbd[j, 0:64, 0:64] = proj[2 * j]
        projbd[j, 64:128, 64:128] = proj[2 * j + 1]
    ii = np.arange(128)
    mask = (ii[:, None] <= ii[None, :]).astype(np.float32)
    mask4 = np.ascontiguousarray(np.tile(mask, (1, 4)))
    ident = np.eye(128, dtype=np.float32)
    bq_cols = np.ascontiguousarray(qkv_b[0:1024].reshape(8, 128).T)
    bk_cols = np.ascontiguousarray(qkv_b[1024:2048].reshape(8, 128).T)
    vbias_bc = np.ascontiguousarray(np.tile(qkv_b[2048:3072][None, :], (128, 1)))
    outb_bc = np.ascontiguousarray(np.tile(out_b[None, :], (128, 1)))
    scale_bc = np.ascontiguousarray(np.tile(fscale[None, :], (128, 1)))
    bias_bc = np.ascontiguousarray(np.tile(fbias[None, :], (128, 1)))
    ffn_wT = np.ascontiguousarray(fw.T)

    in_maps = []
    for c in range(NCORES):
        b, q = divmod(c, 4)
        rm = np.zeros((128, NCORES), np.float32)
        for qq in range(q):
            rm[:, b * 4 + qq] = 1.0
        in_maps.append({
            "x": np.ascontiguousarray(x[b, q * T:(q + 1) * T, :]),
            "qkv_w": qkv_w, "bq_cols": bq_cols, "bk_cols": bk_cols,
            "vbias_bc": vbias_bc, "out_w": out_w, "outb_bc": outb_bc,
            "projbd": projbd, "ffn_wT": ffn_wT, "scale_bc": scale_bc,
            "bias_bc": bias_bc, "mask4": mask4, "ident": ident, "rankmask": rm,
        })
    return in_maps


def kernel(**inputs):
    in_maps = prep_in_maps(**inputs)
    nc = build_program()
    res = run_bass_kernel_spmd(nc, in_maps, core_ids=list(range(NCORES)))
    y = np.empty((B, L, E), np.float32)
    for c in range(NCORES):
        b, q = divmod(c, 4)
        y[b, q * T:(q + 1) * T, :] = res.results[c]["y"]
    if DEBUG_TAPS:
        kernel.tap_results = res.results
    return y


# version 21 (final)
# speedup vs baseline: 1.1623x; 1.1623x over previous
import numpy as np
import ml_dtypes
from contextlib import ExitStack
import concourse.bass as bass
import concourse.mybir as mybir
from concourse.bass_utils import run_bass_kernel_spmd

F32 = mybir.dt.float32
BF16 = mybir.dt.bfloat16
AF = mybir.ActivationFunctionType
ALU = mybir.AluOpType
AX = mybir.AxisListType

B, L, E, H, D = 2, 16384, 1024, 16, 64
NCORES = 8
T = 4096
TT = 256
NTILE = T // TT
C = 128
DEBUG_TAPS = ()
TAP_T = 0

CTX_BANK = [h // 6 for h in range(16)]
CTX_COL = [65 * (h % 6) for h in range(16)]


def build_program(a_tiles=NTILE, exchange=True, b_tiles=NTILE):
    nc = bass.Bass("TRN2", target_bir_lowering=False, debug=False)

    x_h = nc.dram_tensor("x", [T, E], F32, kind="ExternalInput").ap()
    qkv_h = nc.dram_tensor("qkv_w", [E, 3 * E], F32, kind="ExternalInput").ap()
    bq_h = nc.dram_tensor("bq_cols", [128, 8], F32, kind="ExternalInput").ap()
    bk_h = nc.dram_tensor("bk_cols", [128, 8], F32, kind="ExternalInput").ap()
    vb_h = nc.dram_tensor("vbias_bc", [128, E], F32, kind="ExternalInput").ap()
    wo_h = nc.dram_tensor("out_w", [E, E], F32, kind="ExternalInput").ap()
    ob_h = nc.dram_tensor("outb_bc", [128, E], F32, kind="ExternalInput").ap()
    pj_h = nc.dram_tensor("projbd", [8, 128, 128], F32, kind="ExternalInput").ap()
    fw_h = nc.dram_tensor("ffn_wT", [E, E], F32, kind="ExternalInput").ap()
    sc_h = nc.dram_tensor("scale_bc", [128, E], F32, kind="ExternalInput").ap()
    fb_h = nc.dram_tensor("bias_bc", [128, E], F32, kind="ExternalInput").ap()
    mk_h = nc.dram_tensor("mask4", [128, 512], F32, kind="ExternalInput").ap()
    id_h = nc.dram_tensor("ident", [128, 128], F32, kind="ExternalInput").ap()
    rm_h = nc.dram_tensor("rankmask", [128, NCORES], F32, kind="ExternalInput").ap()
    y_h = nc.dram_tensor("y", [T, E], F32, kind="ExternalOutput").ap()

    sp_xn = nc.dram_tensor("sp_xn", [NTILE, 128, 2048], BF16).ap()
    sp_kp = nc.dram_tensor("sp_kp", [NTILE, 128, 2048], BF16).ap()
    sp_kt = nc.dram_tensor("sp_kt", [NTILE, 128, 2048], BF16).ap()
    sp_va = nc.dram_tensor("sp_va", [NTILE, 128, 2080], BF16).ap()
    cc_in = nc.dram_tensor("cc_in", [128, 520], F32).ap()
    cc_out = nc.dram_tensor("cc_out", [NCORES * 128, 520], F32,
                            addr_space="Shared").ap()

    sb = nc.alloc_sbuf_tensor
    wq_sb = sb("wq_sb", [128, 8 * 1024], BF16).ap()
    wk_sb = sb("wk_sb", [128, 8 * 1024], BF16).ap()
    wv_sb = sb("wv_sb", [128, 8 * 1024], BF16).ap()
    wo_sb = sb("wo_sb", [128, 8 * 1024], BF16).ap()
    ws_sb = sb("ws_sb", [128, 8 * 1024], BF16).ap()
    pj_sb = sb("pj_sb", [128, 8 * 128], BF16).ap()
    bq_sb = sb("bq_sb", [128, 8], F32).ap()
    bk_sb = sb("bk_sb", [128, 8], F32).ap()
    vb_sb = sb("vb_sb", [128, E], F32).ap()
    ob_sb = sb("ob_sb", [128, E], F32).ap()
    sc_sb = sb("sc_sb", [128, E], F32).ap()
    fb_sb = sb("fb_sb", [128, E], F32).ap()
    mk_sb = sb("mk_sb", [128, 512], F32).ap()
    id_sb = sb("id_sb", [128, 128], BF16).ap()
    rm_sb = sb("rm_sb", [128, NCORES], F32).ap()
    cst = sb("cst", [128, 2], F32).ap()
    x2 = [sb(f"x2_{i}", [128, 2048], F32).ap() for i in range(2)]
    xnfm2 = [sb(f"xnfm_{i}", [128, 2048], BF16).ap() for i in range(2)]
    kpfm2 = [sb(f"kpfm_{i}", [128, 2048], BF16).ap() for i in range(2)]
    ktm2 = [sb(f"ktm_{i}", [128, 2048], BF16).ap() for i in range(2)]
    vaug2 = [sb(f"vaug_{i}", [128, 2080], BF16).ap() for i in range(2)]
    kq_fm0 = sb("kq_fm", [128, 2048], BF16).ap()
    xn_tm0 = sb("xn_tm", [128, 2048], BF16).ap()
    xn_tm1 = wq_sb[:, 0:2048]
    kq_fm1 = wq_sb[:, 2048:4096]
    xn_tm2 = [xn_tm0, xn_tm1]
    kq_fm2 = [kq_fm0, kq_fm1]
    qp_top = sb("qp_top", [128, 2048], BF16).ap()
    qp_bot = sb("qp_bot", [128, 2048], BF16).ap()
    amt2 = [sb(f"amt_{i}", [128, 2048], BF16).ap() for i in range(2)]
    ctxn = sb("ctxn", [128, 2048], BF16).ap()
    ctxfm = sb("ctxfm", [128, 2048], BF16).ap()
    s_sb = sb("s_sb", [128, 2048], F32).ap()
    g_sb = sb("g_sb", [128, 2048], F32).ap()
    S_run = sb("S_run", [128, 520], F32).ap()
    S_bf = sb("S_bf", [128, 520], BF16).ap()
    Sg = sb("Sg", [128, 520], F32).ap()
    lnscr = sb("lnscr", [128, 1024], F32).ap()
    ln_sb = sb("ln_sb", [128, 32], F32).ap()
    nrm_sb = sb("nrm_sb", [128, 64], F32).ap()

    ps = nc.alloc_psum_tensor
    pm = [ps(f"pm{i}", [128, 512], F32).ap() for i in range(3)]
    pc = [ps(f"pc{i}", [128, 512], F32).ap() for i in range(3)]
    pg = [ps(f"pg{i}", [128, 512], F32).ap() for i in range(2)]
    pm_bf = [p.bitcast(BF16) for p in pm]

    ENG = ("sync", "tensor", "vector", "scalar", "gpsimd")

    class Plan:
        def __init__(self):
            self.q = {e: [] for e in ENG}
            self.cnt = {}
            self.hdl = {}
            self.seen = {e: {} for e in ENG}
            self.r = {}

        def add_sem(self, name, hdl):
            self.hdl[name] = hdl
            self.cnt[name] = 0

        def _wait(self, eng, ev):
            sem, val = ev
            if val <= 0 or self.seen[eng].get(sem, 0) >= val:
                return
            self.seen[eng][sem] = val
            h = self.hdl[sem]
            epts = {"sync": nc.sync, "tensor": nc.tensor, "vector": nc.vector,
                    "scalar": nc.scalar, "gpsimd": nc.gpsimd}
            self.q[eng].append(lambda h=h, v=val, E=epts[eng]: E.wait_ge(h, v))

        def _flat(self, evs, out):
            for ev in evs:
                if ev is None:
                    continue
                if isinstance(ev, tuple) and len(ev) == 2 and isinstance(ev[0], str):
                    out.append(ev)
                else:
                    self._flat(ev, out)
            return out

        def op(self, eng, fn, inc=None, waits=(), key=None, note=None):
            for ev in self._flat(waits, []):
                self._wait(eng, ev)
            if inc is not None:
                sem, delta = inc
                self.cnt[sem] += delta
                val = self.cnt[sem]
                h = self.hdl[sem]

                def run(fn=fn, h=h, d=delta, note=note):
                    ins = fn()
                    if note:
                        ins.annotate(note)
                    return ins.then_inc(h, d)
                self.q[eng].append(run)
                ev = (sem, val)
                if key is not None:
                    self.r[key] = ev
                return ev
            else:
                self.q[eng].append(fn)
                return None

    P = Plan()

    stack = ExitStack()
    SEMS = ["T", "V", "S", "P", "CC", "CON", "WL", "FSL", "SGS", "CCS", "YO"]
    for st in ("LX", "LN", "LP", "LT", "LV", "SX", "SP", "ST", "SV"):
        SEMS += [st + "0", st + "1"]
    for i in range(8):
        SEMS.append(f"TP{i}")
    for name in SEMS:
        P.add_sem(name, stack.enter_context(nc.semaphore(name)))
    tap_n = [0]

    def rdy(key):
        return P.r.get(key)

    tn, vc, sc_e, gp, sy = nc.tensor, nc.vector, nc.scalar, nc.gpsimd, nc.sync

    def tap(name, shape, dtype, src_ap, waits):
        if name not in DEBUG_TAPS:
            return None
        t_h = nc.dram_tensor("tap_" + name, shape, dtype, kind="ExternalOutput").ap()
        sem = f"TP{tap_n[0]}"
        tap_n[0] += 1
        return P.op("sync", lambda t_h=t_h, src=src_ap: sy.dma_start(
            out=t_h[:, :], in_=src), inc=(sem, 16), waits=tuple(waits))

    class Rot:
        def __init__(self):
            self.i = 0
            self.banks = [0, 1, 2]
            self.free = {}

        def take(self):
            j = self.banks[self.i % len(self.banks)]
            self.i += 1
            return j, self.free.get(j)

        def give(self, j, ev):
            self.free[j] = ev

    rot = Rot()
    PMX = pm + pc
    PMX_BF = pm_bf + [p.bitcast(BF16) for p in pc]

    def mm_seq(jobs, wait_first, key):
        n = len(jobs)
        tag = str(key[0]) if isinstance(key, tuple) else str(key)
        for i, (o, l, r, st_, sp_, tp) in enumerate(jobs):
            w = tuple(wait_first) if i == 0 else ()
            P.op("tensor",
                 lambda o=o, l=l, r=r, st=st_, sp=sp_, tp=tp: tn.matmul(
                     o, l, r, start=st, stop=sp, tile_position=tp),
                 inc=("T", 1) if i == n - 1 else None, waits=w,
                 key=key if i == n - 1 else None, note=tag)
        return P.r.get(key)

    for dst, src in ((mk_sb, mk_h), (bq_sb, bq_h), (bk_sb, bk_h), (vb_sb, vb_h),
                     (ob_sb, ob_h), (sc_sb, sc_h), (fb_sb, fb_h), (rm_sb, rm_h)):
        P.op("sync", lambda d=dst, s=src: sy.dma_start(out=d[:, :], in_=s[:, :]),
             inc=("CON", 16))
    ev_consts = ("CON", P.cnt["CON"])

    P.op("gpsimd", lambda: gp.dma_start(out=id_sb[:, :], in_=id_h[:, :]),
         inc=("WL", 16))
    P.op("gpsimd", lambda: gp.dma_start(
        out=pj_sb.rearrange("p (j c) -> p j c", j=8), in_=pj_h.transpose([1, 0, 2])),
        inc=("WL", 16))
    for ic in range(8):
        for dst, s0, s1 in ((wk_sb, 1024, 2048), (wv_sb, 2048, 3072)):
            P.op("gpsimd", lambda ic=ic, dst=dst, s0=s0, s1=s1: gp.dma_start(
                out=dst[:, ic * 1024:(ic + 1) * 1024],
                in_=qkv_h[ic * 128:(ic + 1) * 128, s0:s1]), inc=("WL", 16))
        P.op("gpsimd", lambda ic=ic: gp.dma_start(
            out=wo_sb[:, ic * 1024:(ic + 1) * 1024],
            in_=wo_h[ic * 128:(ic + 1) * 128, :]), inc=("WL", 16))
    ev_wload = ("WL", P.cnt["WL"])

    for ic in range(8):
        evg = P.op("gpsimd", lambda ic=ic: gp.dma_start(
            out=lnscr[:, :], in_=fw_h[ic * 128:(ic + 1) * 128, :]),
            inc=("FSL", 16), waits=(rdy(("sgn", ic - 1)),))
        ev1 = P.op("vector", lambda: vc.tensor_scalar(
            out=lnscr[:, :], in0=lnscr[:, :], scalar1=0.0, scalar2=None,
            op0=ALU.is_ge), inc=("V", 1), waits=(evg,))
        P.op("vector", lambda ic=ic: vc.tensor_scalar(
            out=ws_sb[:, ic * 1024:(ic + 1) * 1024], in0=lnscr[:, :],
            scalar1=2.0, scalar2=-1.0, op0=ALU.mult, op1=ALU.add),
            inc=("V", 1), waits=(ev1,), key=("sgn", ic))
    ev_sgn = rdy(("sgn", 7))

    P.op("vector", lambda: vc.memset(S_run[:, :], 0.0), inc=("V", 1),
         key=("SrunA",))
    P.op("vector", lambda: vc.memset(qp_top[64:128, :], 0.0), inc=("V", 1),
         key=("qpz", 0))
    P.op("vector", lambda: vc.memset(qp_bot[0:64, :], 0.0), inc=("V", 1),
         key=("qpz", 1))
    P.op("vector", lambda: vc.memset(cst[:, 0:1], 1e-5), inc=("V", 1))
    P.op("vector", lambda: vc.memset(cst[:, 1:2], 1e-6), inc=("V", 1))
    for i in range(2):
        P.op("vector", lambda i=i: vc.memset(
            vaug2[i].rearrange("p (t h d) -> p t h d", t=2, h=16)[:, :, :, 64:65],
            1.0), inc=("V", 1), key=("vones", i))
    ev_setup_v = ("V", P.cnt["V"])

    def emit_ln(t):
        bi = t % 2
        lo = 16 * bi
        ev_x = P.op("sync", lambda t=t, bi=bi: sy.dma_start(
            out=x2[bi].rearrange("p (s e) -> p s e", s=2),
            in_=x_h[t * TT:(t + 1) * TT, :].rearrange("(s p) e -> p s e", p=128)),
            inc=(f"LX{bi}", 16), waits=(rdy(("xfree", t - 2)),), key=("xA", t))
        ev_xreads = []
        ev_sq = {}
        ev_su = {}
        for s in range(2):
            sl = slice(s * 1024, (s + 1) * 1024)
            ev_sq[s] = P.op("scalar", lambda sl=sl, s=s, bi=bi, lo=lo:
                            sc_e.activation(
                                out=lnscr[:, :], in_=x2[bi][:, sl], func=AF.Square,
                                accum_out=ln_sb[:, lo + s:lo + s + 1]),
                            inc=("S", 1), waits=(ev_x, ev_sgn))
            ev_su[s] = P.op("vector", lambda sl=sl, s=s, bi=bi, lo=lo: vc.reduce_sum(
                out=ln_sb[:, lo + 2 + s:lo + 3 + s], in_=x2[bi][:, sl], axis=AX.X),
                inc=("V", 1), waits=(ev_x, ev_setup_v))
            ev_xreads.append(ev_su[s])
        ev_mn = P.op("vector", lambda lo=lo: vc.tensor_scalar(
            out=ln_sb[:, lo + 4:lo + 6], in0=ln_sb[:, lo + 2:lo + 4],
            scalar1=1.0 / 1024.0, scalar2=None, op0=ALU.mult),
            inc=("V", 1), waits=(ev_su[0], ev_su[1]))
        ev_v1 = P.op("vector", lambda lo=lo: vc.scalar_tensor_tensor(
            out=ln_sb[:, lo + 6:lo + 8], in0=ln_sb[:, lo + 4:lo + 6], scalar=-1.0,
            in1=ln_sb[:, lo + 4:lo + 6], op0=ALU.mult, op1=ALU.mult),
            inc=("V", 1), waits=(ev_mn,))
        ev_var = P.op("vector", lambda lo=lo: vc.scalar_tensor_tensor(
            out=ln_sb[:, lo + 6:lo + 8], in0=ln_sb[:, lo:lo + 2],
            scalar=1.0 / 1024.0, in1=ln_sb[:, lo + 6:lo + 8],
            op0=ALU.mult, op1=ALU.add),
            inc=("V", 1), waits=(ev_v1, ev_sq[0], ev_sq[1]))
        ev_ln1 = P.op("scalar", lambda lo=lo: sc_e.activation(
            out=ln_sb[:, lo + 8:lo + 10], in_=ln_sb[:, lo + 6:lo + 8], func=AF.Ln,
            bias=cst[:, 0:1], scale=1.0), inc=("S", 1), waits=(ev_var, ev_consts))
        ev_rs = P.op("scalar", lambda lo=lo: sc_e.activation(
            out=ln_sb[:, lo + 8:lo + 10], in_=ln_sb[:, lo + 8:lo + 10],
            func=AF.Exp, scale=-0.5), inc=("S", 1), waits=(ev_ln1,))
        ev_nm1 = P.op("vector", lambda lo=lo: vc.tensor_mul(
            ln_sb[:, lo + 10:lo + 12], ln_sb[:, lo + 4:lo + 6],
            ln_sb[:, lo + 8:lo + 10]), inc=("V", 1), waits=(ev_rs,))
        ev_nm = P.op("vector", lambda lo=lo: vc.tensor_scalar(
            out=ln_sb[:, lo + 10:lo + 12], in0=ln_sb[:, lo + 10:lo + 12],
            scalar1=-1.0, scalar2=None, op0=ALU.mult),
            inc=("V", 1), waits=(ev_nm1,))
        for s in range(2):
            sl = slice(s * 1024, (s + 1) * 1024)
            P.op("scalar", lambda sl=sl, s=s, bi=bi, lo=lo: sc_e.activation(
                out=xn_tm2[bi][:, sl], in_=x2[bi][:, sl], func=AF.Identity,
                bias=ln_sb[:, lo + 10 + s:lo + 11 + s],
                scale=ln_sb[:, lo + 8 + s:lo + 9 + s]),
                inc=("S", 1), waits=(ev_nm, rdy(("xn_used", t - 2))),
                key=("xn", t, s))
            ev_xreads.append(P.r[("xn", t, s)])
        P.r[("xfree", t)] = tuple(ev_xreads)
        P.r[("xAev", t)] = ev_x

    emit_ln(0)
    rot.banks = [0, 1, 2, 3, 4, 5]
    rot.i = 0
    for t in range(a_tiles):
        bi = t % 2
        if t + 1 < a_tiles:
            emit_ln(t + 1)
        ev_x = rdy(("xAev", t))
        ev_tap_xn = tap("xn0", [128, 2048], BF16, xn_tm2[bi],
                        [rdy(("xn", t, 1))]) if t == TAP_T else None
        for s in range(2):
            j3, evf = rot.take()
            for j in range(8):
                w = (rdy(("xn", t, s)), evf, ev_wload) if j == 0 else ()
                P.op("tensor", lambda j3=j3, j=j, s=s, bi=bi: tn.transpose(
                    PMX_BF[j3][:, j * 128:(j + 1) * 128],
                    xn_tm2[bi][:, s * 1024 + j * 128: s * 1024 + (j + 1) * 128],
                    id_sb[:, :]),
                    inc=("T", 1) if j == 7 else None, waits=w,
                    key=("xnT", t, s) if j == 7 else None)
            ev_cp = P.op("vector", lambda s=s, j3=j3, bi=bi: vc.tensor_copy(
                out=xnfm2[bi].rearrange("p (j c) -> p j c", j=8)[:, :, s * 128:(s + 1) * 128],
                in_=PMX_BF[j3].rearrange("p (j c) -> p j c", j=8)),
                inc=("V", 1), waits=(rdy(("xnT", t, s)), rdy(("xnfm_used", t - 2))),
                key=("xnfm", t, s))
            rot.give(j3, ev_cp)
        ev_xnfm = (rdy(("xnfm", t, 0)), rdy(("xnfm", t, 1)))
        P.r[("xn_used", t)] = (rdy(("xnfm", t, 1)), ev_tap_xn)
        ev_spxn = P.op("gpsimd", lambda t=t, bi=bi: gp.dma_start(
            out=sp_xn[t], in_=xnfm2[bi][:, :]), inc=(f"SX{bi}", 16),
            waits=(ev_xnfm,))
        for kc in range(8):
            j3, evf = rot.take()
            bank = PMX[j3]
            jobs = [(bank[:, 0:256],
                     wk_sb[:, ic * 1024 + kc * 128: ic * 1024 + (kc + 1) * 128],
                     xnfm2[bi][:, ic * 256:(ic + 1) * 256],
                     ic == 0, ic == 7, None) for ic in range(8)]
            mm_seq(jobs, (ev_xnfm, evf), ("kmm", t, kc))
            ev_ev = P.op("scalar", lambda kc=kc, bank=bank, bi=bi: sc_e.activation(
                out=kq_fm2[bi][:, kc * 256:(kc + 1) * 256], in_=bank[:, 0:256],
                func=AF.Identity, bias=bk_sb[:, kc:kc + 1], scale=1.0),
                inc=("S", 1), waits=(rdy(("kmm", t, kc)), rdy(("kq_used", t - 2)),
                                     ev_consts),
                key=("kfm", t, kc))
            rot.give(j3, ev_ev)
        ev_kfm = rdy(("kfm", t, 7))
        ev_tap_kq = tap("kq0", [128, 2048], BF16, kq_fm2[bi],
                        [ev_kfm]) if t == TAP_T else None
        for hp in range(8):
            j3, evf = rot.take()
            bank = PMX[j3]
            jobs = [(bank[:, 0:256], pj_sb[:, hp * 128:(hp + 1) * 128],
                     kq_fm2[bi][:, hp * 256:(hp + 1) * 256], True, True, None)]
            mm_seq(jobs, (rdy(("kfm", t, hp)), evf), ("kpmm", t, hp))
            ev_ev = P.op("scalar", lambda hp=hp, bank=bank, bi=bi: sc_e.activation(
                out=kpfm2[bi][:, hp * 256:(hp + 1) * 256], in_=bank[:, 0:256],
                func=AF.Relu), inc=("S", 1),
                waits=(rdy(("kpmm", t, hp)), rdy(("kpfm_used", t - 2))),
                key=("kpfm", t, hp))
            rot.give(j3, ev_ev)
        ev_kpfm = tuple(rdy(("kpfm", t, i)) for i in range(8))
        ev_tap_kp = tap("kpfm0", [128, 2048], BF16, kpfm2[bi],
                        [ev_kpfm]) if t == TAP_T else None
        ev_spkp = P.op("gpsimd", lambda t=t, bi=bi: gp.dma_start(
            out=sp_kp[t], in_=kpfm2[bi][:, :]), inc=(f"SP{bi}", 16),
            waits=(ev_kpfm,))
        for tc in range(2):
            for half in range(2):
                j3, evf = rot.take()
                bank = PMX[j3]
                jobs = []
                for k4 in range(4):
                    kc = half * 4 + k4
                    jobs.append((bank[:, k4 * 128:(k4 + 1) * 128],
                                 kq_fm2[bi][:, kc * 256 + tc * 128: kc * 256 + tc * 128 + 128],
                                 pj_sb[:, kc * 128:(kc + 1) * 128],
                                 True, True, None))
                mm_seq(jobs, (ev_kfm, evf), ("ktmm", t, tc, half))
                ev_ev = P.op("scalar", lambda tc=tc, half=half, bank=bank, bi=bi:
                             sc_e.activation(
                                 out=ktm2[bi][:, tc * 1024 + half * 512:
                                              tc * 1024 + half * 512 + 512],
                                 in_=bank[:, :], func=AF.Relu), inc=("S", 1),
                             waits=(rdy(("ktmm", t, tc, half)),
                                    rdy(("ktm_used", t - 2))),
                             key=("ktm", t, tc, half))
                rot.give(j3, ev_ev)
        ev_ktm = tuple(rdy(("ktm", t, a, b)) for a in range(2) for b in range(2))
        ev_tap_kt = tap("ktm0", [128, 2048], BF16, ktm2[bi],
                        [ev_ktm]) if t == TAP_T else None
        ev_spkt = P.op("gpsimd", lambda t=t, bi=bi: gp.dma_start(
            out=sp_kt[t], in_=ktm2[bi][:, :]), inc=(f"ST{bi}", 16),
            waits=(ev_ktm,))
        for tc in range(2):
            for n in range(2):
                j3, evf = rot.take()
                bank = PMX[j3]
                jobs = [(bank[:, :],
                         xnfm2[bi][:, ic * 256 + tc * 128: ic * 256 + tc * 128 + 128],
                         wv_sb[:, ic * 1024 + n * 512: ic * 1024 + (n + 1) * 512],
                         ic == 0, ic == 7, None) for ic in range(8)]
                mm_seq(jobs, (ev_xnfm, evf), ("vmm", t, tc, n))
                ev_ev = P.op("vector", lambda tc=tc, n=n, bank=bank, bi=bi:
                             vc.tensor_tensor(
                                 out=vaug2[bi].rearrange(
                                     "p (tc h d) -> p tc h d", tc=2, h=16
                                 )[:, tc, n * 8:(n + 1) * 8, 0:64],
                                 in0=bank.rearrange("p (h d) -> p h d", h=8),
                                 in1=vb_sb.rearrange("p (h d) -> p h d", h=16)[:, n * 8:(n + 1) * 8, :],
                                 op=ALU.add), inc=("V", 1),
                             waits=(rdy(("vmm", t, tc, n)), rdy(("vaug_used", t - 2)),
                                    rdy(("vones", bi)), ev_consts),
                             key=("vaug", t, tc, n))
                rot.give(j3, ev_ev)
        ev_vaug = tuple(rdy(("vaug", t, a, b)) for a in range(2) for b in range(2))
        ev_spva = P.op("gpsimd", lambda t=t, bi=bi: gp.dma_start(
            out=sp_va[t], in_=vaug2[bi][:, :]), inc=(f"SV{bi}", 16),
            waits=(ev_vaug,))
        ev_tap_va = tap("vaug0", [128, 2080], BF16, vaug2[bi],
                        [ev_vaug]) if t == TAP_T else None
        P.r[("xnfm_used", t)] = (("T", P.cnt["T"]), ev_spxn)
        P.r[("kq_used", t)] = (("T", P.cnt["T"]), ev_tap_kq)
        for tc in range(2):
            jobs0, jobs1 = [], []
            for h in range(H):
                j = h // 2
                bank = pg[0] if j < 4 else pg[1]
                jl = jobs0 if j < 4 else jobs1
                jl.append((bank[64 * (h % 2):64 * (h % 2) + 64,
                                65 * (j % 4):65 * (j % 4) + 65],
                           ktm2[bi][:, tc * 1024 + j * 128 + (h % 2) * 64:
                                    tc * 1024 + j * 128 + (h % 2) * 64 + 64],
                           vaug2[bi][:, tc * 1040 + h * 65: tc * 1040 + h * 65 + 65],
                           True, True, (0, 64 * (h % 2))))
            for bank_i, jl in ((0, jobs0), (1, jobs1)):
                mm_seq(jl, (rdy(("ktm", t, tc, 0)), rdy(("ktm", t, tc, 1)),
                            rdy(("vaug", t, tc, 0)), rdy(("vaug", t, tc, 1)),
                            rdy(("pgfA", bank_i))), ("dsA", t, tc, bank_i))
            P.op("vector", lambda: vc.tensor_tensor(
                out=S_run[:, 0:260], in0=S_run[:, 0:260], in1=pg[0][:, 0:260],
                op=ALU.add), inc=("V", 1),
                waits=(rdy(("dsA", t, tc, 0)), rdy(("SrunA",))), key=("pgfA", 0))
            P.op("vector", lambda: vc.tensor_tensor(
                out=S_run[:, 260:520], in0=S_run[:, 260:520], in1=pg[1][:, 0:260],
                op=ALU.add), inc=("V", 1),
                waits=(rdy(("dsA", t, tc, 1)),), key=("pgfA", 1))
        P.r[("ktm_used", t)] = (("T", P.cnt["T"]), ev_spkt, ev_tap_kt)
        P.r[("vaug_used", t)] = (("T", P.cnt["T"]), ev_spva, ev_tap_va)
        P.r[("kpfm_used", t)] = (ev_spkp, ev_tap_kp)
        P.r[("spill", t)] = (ev_spxn, ev_spkp, ev_spkt, ev_spva)

    if not exchange:
        for i in range(tap_n[0]):
            v = P.cnt[f"TP{i}"]
            P.q["sync"].append(lambda i=i, v=v: sy.wait_ge(P.hdl[f"TP{i}"], v))
        with stack:
            with nc.Block() as block:
                block.sync(lambda e: [f() for f in P.q["sync"]])
                block.tensor(lambda e: [f() for f in P.q["tensor"]])
                block.vector(lambda e: [f() for f in P.q["vector"]])
                block.scalar(lambda e: [f() for f in P.q["scalar"]])
                block.gpsimd(lambda e: [f() for f in P.q["gpsimd"]])
        return nc

    for cb in range(3):
        P.r[("pcf", cb)] = rot.free.get(3 + cb)
    rot.banks = [0, 1, 2]
    rot.i = 0
    ev_sl0 = rdy(("pgfA", 0))
    ev_sl1 = rdy(("pgfA", 1))
    ev_phA = (("T", P.cnt["T"]), ("S", P.cnt["S"]), ("V", P.cnt["V"]))
    ev_wq = None
    for ic in range(8):
        ev_wq = P.op("gpsimd", lambda ic=ic: gp.dma_start(
            out=wq_sb[:, ic * 1024:(ic + 1) * 1024],
            in_=qkv_h[ic * 128:(ic + 1) * 128, 0:1024]), inc=("WL", 16),
            waits=(ev_phA,) if ic == 0 else ())
    P.r[("wq",)] = ev_wq
    ev_tap_sloc = tap("sloc", [128, 520], F32, S_run, [ev_sl0, ev_sl1])
    ev_ccw = P.op("gpsimd", lambda: gp.dma_start(out=cc_in[:, :], in_=S_run[:, :]),
                  inc=("CCS", 16), waits=(ev_sl0, ev_sl1))
    ev_cc = P.op("gpsimd", lambda: gp.collective_compute(
        "AllGather", ALU.bypass, replica_groups=[list(range(NCORES))],
        ins=[cc_in[:, :]], outs=[cc_out[:, :]]), inc=("CC", 1), waits=(ev_ccw,))
    ev_z = P.op("vector", lambda: vc.memset(S_run[:, :], 0.0), inc=("V", 1),
                waits=(ev_ccw, ev_tap_sloc))
    prev = ev_z
    for r in range(NCORES):
        ev_g = P.op("gpsimd", lambda r=r: gp.dma_start(
            out=Sg[:, :], in_=cc_out[r * 128:(r + 1) * 128, :]),
            inc=("SGS", 16), waits=(ev_cc, prev))
        prev = P.op("vector", lambda r=r: vc.scalar_tensor_tensor(
            out=S_run[:, :], in0=Sg[:, :], scalar=rm_sb[:, r:r + 1],
            in1=S_run[:, :], op0=ALU.mult, op1=ALU.add),
            inc=("V", 1), waits=(ev_g, prev))
    ev_sinit = prev
    ev_tap_sinit = tap("sinit", [128, 520], F32, S_run, [ev_sinit])
    ev_sinit = (ev_sinit, ev_tap_sinit)
    ev_sbf = P.op("vector", lambda: vc.tensor_copy(out=S_bf[:, :], in_=S_run[:, :]),
                  inc=("V", 1), waits=(ev_sinit,))
    P.r[("Sbf",)] = ev_sbf
    P.r[("pgf", 0)] = ev_sl0
    P.r[("pgf", 1)] = ev_sl1

    def emit_chunk_at(t, bi, tc):
        am = amt2[tc]
        for grp in range(4):
            j3, evf = rot.take()
            bank = pm[j3]
            jobs = []
            for hh in range(4):
                h = grp * 4 + hh
                hp, hf = h // 2, h % 2
                off = hp * 256 + tc * 128
                qz = qp_top if hf == 0 else qp_bot
                jobs.append((bank[:, hh * 128:(hh + 1) * 128],
                             kpfm2[bi][:, off:off + 128],
                             qz[:, off:off + 128], True, True, None))
            mm_seq(jobs, (rdy(("qpok", t)), rdy(("lkp", t)), evf),
                   ("atmm", t, tc, grp))
            ev_ev = P.op("vector", lambda grp=grp, bank=bank, am=am: vc.tensor_tensor(
                out=am[:, grp * 512:(grp + 1) * 512], in0=bank[:, :],
                in1=mk_sb[:, :], op=ALU.mult), inc=("V", 1),
                waits=(rdy(("atmm", t, tc, grp)), rdy(("amt_used", tc, grp)),
                       ev_consts),
                key=("amt", t, tc, grp))
            rot.give(j3, ev_ev)

    def emit_chunk_mid(t, bi, tc):
        am = amt2[tc]
        ev_amt = tuple(rdy(("amt", t, tc, g)) for g in range(4))
        for cb in range(3):
            jobs = []
            for h in range(16):
                if CTX_BANK[h] != cb:
                    continue
                hp, hf = h // 2, h % 2
                off = hp * 256 + tc * 128
                qz = qp_top if hf == 0 else qp_bot
                sco = 260 * (hp // 4) + 65 * (hp % 4)
                jobs.append((pc[cb][:, CTX_COL[h]:CTX_COL[h] + 65],
                             am[:, h * 128:(h + 1) * 128],
                             vaug2[bi][:, tc * 1040 + h * 65: tc * 1040 + h * 65 + 65],
                             True, False, None))
                jobs.append((pc[cb][:, CTX_COL[h]:CTX_COL[h] + 65],
                             qz[:, off:off + 128],
                             S_bf[:, sco:sco + 65], False, True, None))
            mm_seq(jobs, ev_amt + (rdy(("lva", t)), rdy(("Sbf",)), rdy(("pcf", cb))),
                   ("ctx", t, tc, cb))
        ev_ctx_t = ("T", P.cnt["T"])
        for g in range(4):
            P.r[("amt_used", tc, g)] = ev_ctx_t
        for bank_i in range(2):
            jobs = []
            for j4 in range(4):
                j = bank_i * 4 + j4
                for hf in range(2):
                    h = 2 * j + hf
                    jobs.append((pg[bank_i][64 * hf:64 * hf + 64,
                                            65 * j4:65 * j4 + 65],
                                 ktm2[bi][:, tc * 1024 + j * 128 + hf * 64:
                                          tc * 1024 + j * 128 + hf * 64 + 64],
                                 vaug2[bi][:, tc * 1040 + h * 65:
                                           tc * 1040 + h * 65 + 65],
                                 True, True, (0, 64 * hf)))
            mm_seq(jobs, (rdy(("lkt", t)), rdy(("lva", t)), rdy(("pgf", bank_i))),
                   ("dsB", t, tc, bank_i))
        nco = 32 * tc
        ev_nc = []
        for cb in range(3):
            nsl = [6, 6, 4][cb]
            no = [0, 6, 12][cb]
            ev_nc.append(P.op("vector", lambda cb=cb, nsl=nsl, no=no, nco=nco:
                              vc.tensor_copy(
                                  out=nrm_sb[:, nco + no: nco + no + nsl],
                                  in_=pc[cb][:, 0:65 * nsl].rearrange(
                                      "p (h d) -> p h d", d=65)[:, :, 64]),
                              inc=("V", 1), waits=(rdy(("ctx", t, tc, cb)),)))
        ev_na = P.op("vector", lambda nco=nco: vc.tensor_scalar(
            out=nrm_sb[:, nco + 16:nco + 32], in0=nrm_sb[:, nco:nco + 16],
            scalar1=1e-6, scalar2=None, op0=ALU.add),
            inc=("V", 1), waits=tuple(ev_nc))
        ev_rn = P.op("vector", lambda nco=nco: vc.reciprocal(
            out=nrm_sb[:, nco + 16:nco + 32], in_=nrm_sb[:, nco + 16:nco + 32]),
            inc=("V", 1), waits=(ev_na,))
        for cb in range(3):
            nsl = [6, 6, 4][cb]
            no = [0, 6, 12][cb]
            ev_ev = P.op("vector", lambda cb=cb, tc=tc, nsl=nsl, no=no, nco=nco:
                         vc.tensor_tensor(
                             out=ctxn[:, tc * 1024 + no * 64:
                                      tc * 1024 + (no + nsl) * 64]
                             .rearrange("p (h d) -> p h d", d=64),
                             in0=pc[cb][:, 0:65 * nsl].rearrange(
                                 "p (h d) -> p h d", d=65)[:, :, 0:64],
                             in1=nrm_sb[:, nco + 16 + no: nco + 16 + no + nsl][:, :, None]
                             .broadcast_to([128, nsl, 64]),
                             op=ALU.mult), inc=("V", 1),
                         waits=(ev_rn, rdy(("ctxn_used", tc))),
                         key=("ctxn", t, tc, cb))
            P.r[("pcf", cb)] = ev_ev
        ev_a0 = P.op("vector", lambda: vc.tensor_tensor(
            out=S_run[:, 0:260], in0=S_run[:, 0:260], in1=pg[0][:, 0:260],
            op=ALU.add), inc=("V", 1),
            waits=(rdy(("dsB", t, tc, 0)), ev_sinit))
        ev_a1 = P.op("vector", lambda: vc.tensor_tensor(
            out=S_run[:, 260:520], in0=S_run[:, 260:520], in1=pg[1][:, 0:260],
            op=ALU.add), inc=("V", 1), waits=(rdy(("dsB", t, tc, 1)),))
        P.r[("pgf", 0)] = ev_a0
        P.r[("pgf", 1)] = ev_a1
        ev_sbf2 = P.op("vector", lambda: vc.tensor_copy(
            out=S_bf[:, :], in_=S_run[:, :]), inc=("V", 1),
            waits=(ev_a0, ev_a1, ev_ctx_t))
        P.r[("Sbf",)] = ev_sbf2

    def emit_chunk_back(t, tc):
        ev_ctxn = tuple(rdy(("ctxn", t, tc, cb)) for cb in range(3))
        j3, evf = rot.take()
        for i in range(8):
            w = ev_ctxn + (evf,) if i == 0 else ()
            P.op("tensor", lambda i=i, tc=tc, j3=j3: tn.transpose(
                pm_bf[j3][:, i * 128:(i + 1) * 128],
                ctxn[:, tc * 1024 + i * 128: tc * 1024 + (i + 1) * 128],
                id_sb[:, :]),
                inc=("T", 1) if i == 7 else None, waits=w,
                key=("ctxT", t, tc) if i == 7 else None)
        ev_cf = P.op("vector", lambda tc=tc, j3=j3: vc.tensor_copy(
            out=ctxfm.rearrange("p (j c) -> p j c", j=8)[:, :, tc * 128:(tc + 1) * 128],
            in_=pm_bf[j3].rearrange("p (j c) -> p j c", j=8)),
            inc=("V", 1), waits=(rdy(("ctxT", t, tc)), rdy(("ctxfm_used", t - 1))),
            key=("ctxfm", t, tc))
        rot.give(j3, ev_cf)
        P.r[("ctxn_used", tc)] = rdy(("ctxT", t, tc))

    def emit_flow(key, lhs_list, rhs_list, wait_extra, evac):
        j3, evf = rot.take()
        bank = pm[j3]
        jobs = [(bank[:, :], lhs_list[ic], rhs_list[ic], ic == 0, ic == 7, None)
                for ic in range(8)]
        mm_seq(jobs, tuple(wait_extra) + (evf,), key)
        ev = evac(bank)
        rot.give(j3, ev)
        return ev

    def emit_epilogue(t):
        bi = t % 2
        for tc in range(2):
            for n in range(2):
                def evac_s(bank, t=t, tc=tc, n=n):
                    return P.op("vector", lambda tc=tc, n=n, bank=bank:
                                vc.tensor_tensor(
                                    out=s_sb[:, tc * 1024 + n * 512:
                                             tc * 1024 + n * 512 + 512],
                                    in0=bank[:, :],
                                    in1=ob_sb[:, n * 512:(n + 1) * 512],
                                    op=ALU.add), inc=("V", 1),
                                waits=(rdy(("smm", t, tc, n)),
                                       rdy(("s_out", t - 1))),
                                key=("s", t, tc, n))

                emit_flow(("smm", t, tc, n),
                          [ctxfm[:, ic * 256 + tc * 128: ic * 256 + tc * 128 + 128]
                           for ic in range(8)],
                          [wo_sb[:, ic * 1024 + n * 512: ic * 1024 + (n + 1) * 512]
                           for ic in range(8)],
                          (rdy(("ctxfm", t, tc)),), evac_s)
        P.r[("ctxfm_used", t)] = ("T", P.cnt["T"])
        ev_tap_s = tap("s0", [128, 2048], F32, s_sb,
                       [rdy(("s", 0, tcc, n)) for tcc in range(2)
                        for n in range(2)]) if t == 0 else None
        ev_fin = []
        for tc in range(2):
            for n in range(2):
                sl = slice(tc * 1024 + n * 512, tc * 1024 + n * 512 + 512)
                ev_m = P.op("gpsimd", lambda sl=sl: gp.tensor_tensor(
                    out=g_sb[:, sl], in0=g_sb[:, sl], in1=s_sb[:, sl],
                    op=ALU.mult), inc=("P", 1),
                    waits=(rdy(("g", t, tc, n)), rdy(("s", t, tc, n))))
                ev_f = P.op("gpsimd", lambda sl=sl, bi=bi: gp.tensor_tensor(
                    out=s_sb[:, sl], in0=g_sb[:, sl], in1=x2[bi][:, sl],
                    op=ALU.add), inc=("P", 1), waits=(ev_m, rdy(("xB", t))))
                ev_fin.append(ev_f)
        P.r[("g_out", t)] = tuple(ev_fin)
        ev_tap_sg = tap("sg0", [128, 2048], F32, s_sb,
                        list(ev_fin)) if t == 0 else None
        ev_st = P.op("sync", lambda t=t: sy.dma_start(
            out=y_h[t * TT:(t + 1) * TT, :].rearrange("(s p) e -> p s e", p=128),
            in_=s_sb.rearrange("p (s e) -> p s e", s=2)),
            inc=("YO", 16), waits=tuple(ev_fin), key=("yout", t))
        P.r[("s_out", t)] = (ev_st, ev_tap_s, ev_tap_sg)
        P.r[("xBfree", t)] = tuple(ev_fin)

    def emit_xload(t):
        bi = t % 2
        P.op("sync", lambda t=t, bi=bi: sy.dma_start(
            out=x2[bi].rearrange("p (s e) -> p s e", s=2),
            in_=x_h[t * TT:(t + 1) * TT, :].rearrange("(s p) e -> p s e", p=128)),
            inc=(f"LX{bi}", 16),
            waits=(rdy(("xBfree", t - 2)), rdy(("xfree", a_tiles - 1))),
            key=("xB", t))

    def emit_loads(t):
        bi = t % 2
        P.op("sync", lambda t=t, bi=bi: sy.dma_start(
            out=xnfm2[bi][:, :], in_=sp_xn[t]), inc=(f"LN{bi}", 16),
            waits=(rdy(("xnfmB_used", t - 2)), rdy(("xnfm_used", a_tiles - 1)),
                   rdy(("spill", t))), key=("lxn", t))
        P.op("sync", lambda t=t, bi=bi: sy.dma_start(
            out=kpfm2[bi][:, :], in_=sp_kp[t]), inc=(f"LP{bi}", 16),
            waits=(rdy(("kpfmB_used", t - 2)), rdy(("kpfm_used", a_tiles - 1)),
                   rdy(("spill", t))), key=("lkp", t))
        P.op("sync", lambda t=t, bi=bi: sy.dma_start(
            out=ktm2[bi][:, :], in_=sp_kt[t]), inc=(f"LT{bi}", 16),
            waits=(rdy(("ktmB_used", t - 2)), rdy(("ktm_used", a_tiles - 1)),
                   rdy(("spill", t))), key=("lkt", t))
        P.op("sync", lambda t=t, bi=bi: sy.dma_start(
            out=vaug2[bi][:, :], in_=sp_va[t]), inc=(f"LV{bi}", 16),
            waits=(rdy(("vaugB_used", t - 2)), rdy(("vaug_used", a_tiles - 1)),
                   rdy(("spill", t))), key=("lva", t))

    for tt in range(min(2, b_tiles)):
        emit_xload(tt)
        emit_loads(tt)
    for t in range(b_tiles):
        bi = t % 2
        ev_x = rdy(("xB", t))
        ev_lxn = rdy(("lxn", t))
        if t > 0:
            emit_chunk_back(t - 1, 1)
            emit_epilogue(t - 1)
            if t + 1 < b_tiles:
                emit_xload(t + 1)
        for qc in (0,):
            j3, evf = rot.take()
            bank = pm[j3]
            jobs = [(bank[:, 0:256],
                     wq_sb[:, ic * 1024 + qc * 128: ic * 1024 + (qc + 1) * 128],
                     xnfm2[bi][:, ic * 256:(ic + 1) * 256],
                     ic == 0, ic == 7, None) for ic in range(8)]
            mm_seq(jobs, (ev_lxn, evf, rdy(("wq",))), ("qmm", t, qc))
            ev_ev = P.op("scalar", lambda qc=qc, bank=bank: sc_e.activation(
                out=kq_fm0[:, qc * 256:(qc + 1) * 256], in_=bank[:, 0:256],
                func=AF.Identity, bias=bq_sb[:, qc:qc + 1], scale=1.0),
                inc=("S", 1), waits=(rdy(("qmm", t, qc)), rdy(("kqB_used", t - 1))),
                key=("qfm", t, qc))
            rot.give(j3, ev_ev)
        for qc in (1,):
            j3, evf = rot.take()
            bank = pm[j3]
            jobs = [(bank[:, 0:256],
                     wq_sb[:, ic * 1024 + qc * 128: ic * 1024 + (qc + 1) * 128],
                     xnfm2[bi][:, ic * 256:(ic + 1) * 256],
                     ic == 0, ic == 7, None) for ic in range(8)]
            mm_seq(jobs, (ev_lxn, evf, rdy(("wq",))), ("qmm", t, qc))
            ev_ev = P.op("scalar", lambda qc=qc, bank=bank: sc_e.activation(
                out=kq_fm0[:, qc * 256:(qc + 1) * 256], in_=bank[:, 0:256],
                func=AF.Identity, bias=bq_sb[:, qc:qc + 1], scale=1.0),
                inc=("S", 1), waits=(rdy(("qmm", t, qc)), rdy(("kqB_used", t - 1))),
                key=("qfm", t, qc))
            rot.give(j3, ev_ev)
        for qc in (2,):
            j3, evf = rot.take()
            bank = pm[j3]
            jobs = [(bank[:, 0:256],
                     wq_sb[:, ic * 1024 + qc * 128: ic * 1024 + (qc + 1) * 128],
                     xnfm2[bi][:, ic * 256:(ic + 1) * 256],
                     ic == 0, ic == 7, None) for ic in range(8)]
            mm_seq(jobs, (ev_lxn, evf, rdy(("wq",))), ("qmm", t, qc))
            ev_ev = P.op("scalar", lambda qc=qc, bank=bank: sc_e.activation(
                out=kq_fm0[:, qc * 256:(qc + 1) * 256], in_=bank[:, 0:256],
                func=AF.Identity, bias=bq_sb[:, qc:qc + 1], scale=1.0),
                inc=("S", 1), waits=(rdy(("qmm", t, qc)), rdy(("kqB_used", t - 1))),
                key=("qfm", t, qc))
            rot.give(j3, ev_ev)
        for hp in (0,):
            j3, evf = rot.take()
            bank = pm[j3]
            jobs = [(bank[:, 0:256], pj_sb[:, hp * 128:(hp + 1) * 128],
                     kq_fm0[:, hp * 256:(hp + 1) * 256], True, True, None)]
            mm_seq(jobs, (rdy(("qfm", t, hp)), evf), ("qpmm", t, hp))
            P.op("scalar", lambda hp=hp, bank=bank: sc_e.activation(
                out=qp_top[0:64, hp * 256:(hp + 1) * 256], in_=bank[0:64, 0:256],
                func=AF.Relu), inc=("S", 1),
                waits=(rdy(("qpmm", t, hp)), rdy(("qp_used", t - 1)),
                       rdy(("qpz", 0)), rdy(("qpz", 1))))
            ev_ev = P.op("scalar", lambda hp=hp, bank=bank: sc_e.activation(
                out=qp_bot[64:128, hp * 256:(hp + 1) * 256], in_=bank[64:128, 0:256],
                func=AF.Relu), inc=("S", 1), key=("qp", t, hp))
            rot.give(j3, ev_ev)
        for qc in (3,):
            j3, evf = rot.take()
            bank = pm[j3]
            jobs = [(bank[:, 0:256],
                     wq_sb[:, ic * 1024 + qc * 128: ic * 1024 + (qc + 1) * 128],
                     xnfm2[bi][:, ic * 256:(ic + 1) * 256],
                     ic == 0, ic == 7, None) for ic in range(8)]
            mm_seq(jobs, (ev_lxn, evf, rdy(("wq",))), ("qmm", t, qc))
            ev_ev = P.op("scalar", lambda qc=qc, bank=bank: sc_e.activation(
                out=kq_fm0[:, qc * 256:(qc + 1) * 256], in_=bank[:, 0:256],
                func=AF.Identity, bias=bq_sb[:, qc:qc + 1], scale=1.0),
                inc=("S", 1), waits=(rdy(("qmm", t, qc)), rdy(("kqB_used", t - 1))),
                key=("qfm", t, qc))
            rot.give(j3, ev_ev)
        for hp in (1,):
            j3, evf = rot.take()
            bank = pm[j3]
            jobs = [(bank[:, 0:256], pj_sb[:, hp * 128:(hp + 1) * 128],
                     kq_fm0[:, hp * 256:(hp + 1) * 256], True, True, None)]
            mm_seq(jobs, (rdy(("qfm", t, hp)), evf), ("qpmm", t, hp))
            P.op("scalar", lambda hp=hp, bank=bank: sc_e.activation(
                out=qp_top[0:64, hp * 256:(hp + 1) * 256], in_=bank[0:64, 0:256],
                func=AF.Relu), inc=("S", 1),
                waits=(rdy(("qpmm", t, hp)), rdy(("qp_used", t - 1)),
                       rdy(("qpz", 0)), rdy(("qpz", 1))))
            ev_ev = P.op("scalar", lambda hp=hp, bank=bank: sc_e.activation(
                out=qp_bot[64:128, hp * 256:(hp + 1) * 256], in_=bank[64:128, 0:256],
                func=AF.Relu), inc=("S", 1), key=("qp", t, hp))
            rot.give(j3, ev_ev)
        for qc in (4,):
            j3, evf = rot.take()
            bank = pm[j3]
            jobs = [(bank[:, 0:256],
                     wq_sb[:, ic * 1024 + qc * 128: ic * 1024 + (qc + 1) * 128],
                     xnfm2[bi][:, ic * 256:(ic + 1) * 256],
                     ic == 0, ic == 7, None) for ic in range(8)]
            mm_seq(jobs, (ev_lxn, evf, rdy(("wq",))), ("qmm", t, qc))
            ev_ev = P.op("scalar", lambda qc=qc, bank=bank: sc_e.activation(
                out=kq_fm0[:, qc * 256:(qc + 1) * 256], in_=bank[:, 0:256],
                func=AF.Identity, bias=bq_sb[:, qc:qc + 1], scale=1.0),
                inc=("S", 1), waits=(rdy(("qmm", t, qc)), rdy(("kqB_used", t - 1))),
                key=("qfm", t, qc))
            rot.give(j3, ev_ev)
        for hp in (2,):
            j3, evf = rot.take()
            bank = pm[j3]
            jobs = [(bank[:, 0:256], pj_sb[:, hp * 128:(hp + 1) * 128],
                     kq_fm0[:, hp * 256:(hp + 1) * 256], True, True, None)]
            mm_seq(jobs, (rdy(("qfm", t, hp)), evf), ("qpmm", t, hp))
            P.op("scalar", lambda hp=hp, bank=bank: sc_e.activation(
                out=qp_top[0:64, hp * 256:(hp + 1) * 256], in_=bank[0:64, 0:256],
                func=AF.Relu), inc=("S", 1),
                waits=(rdy(("qpmm", t, hp)), rdy(("qp_used", t - 1)),
                       rdy(("qpz", 0)), rdy(("qpz", 1))))
            ev_ev = P.op("scalar", lambda hp=hp, bank=bank: sc_e.activation(
                out=qp_bot[64:128, hp * 256:(hp + 1) * 256], in_=bank[64:128, 0:256],
                func=AF.Relu), inc=("S", 1), key=("qp", t, hp))
            rot.give(j3, ev_ev)
        for qc in (5,):
            j3, evf = rot.take()
            bank = pm[j3]
            jobs = [(bank[:, 0:256],
                     wq_sb[:, ic * 1024 + qc * 128: ic * 1024 + (qc + 1) * 128],
                     xnfm2[bi][:, ic * 256:(ic + 1) * 256],
                     ic == 0, ic == 7, None) for ic in range(8)]
            mm_seq(jobs, (ev_lxn, evf, rdy(("wq",))), ("qmm", t, qc))
            ev_ev = P.op("scalar", lambda qc=qc, bank=bank: sc_e.activation(
                out=kq_fm0[:, qc * 256:(qc + 1) * 256], in_=bank[:, 0:256],
                func=AF.Identity, bias=bq_sb[:, qc:qc + 1], scale=1.0),
                inc=("S", 1), waits=(rdy(("qmm", t, qc)), rdy(("kqB_used", t - 1))),
                key=("qfm", t, qc))
            rot.give(j3, ev_ev)
        for hp in (3,):
            j3, evf = rot.take()
            bank = pm[j3]
            jobs = [(bank[:, 0:256], pj_sb[:, hp * 128:(hp + 1) * 128],
                     kq_fm0[:, hp * 256:(hp + 1) * 256], True, True, None)]
            mm_seq(jobs, (rdy(("qfm", t, hp)), evf), ("qpmm", t, hp))
            P.op("scalar", lambda hp=hp, bank=bank: sc_e.activation(
                out=qp_top[0:64, hp * 256:(hp + 1) * 256], in_=bank[0:64, 0:256],
                func=AF.Relu), inc=("S", 1),
                waits=(rdy(("qpmm", t, hp)), rdy(("qp_used", t - 1)),
                       rdy(("qpz", 0)), rdy(("qpz", 1))))
            ev_ev = P.op("scalar", lambda hp=hp, bank=bank: sc_e.activation(
                out=qp_bot[64:128, hp * 256:(hp + 1) * 256], in_=bank[64:128, 0:256],
                func=AF.Relu), inc=("S", 1), key=("qp", t, hp))
            rot.give(j3, ev_ev)
        for qc in (6,):
            j3, evf = rot.take()
            bank = pm[j3]
            jobs = [(bank[:, 0:256],
                     wq_sb[:, ic * 1024 + qc * 128: ic * 1024 + (qc + 1) * 128],
                     xnfm2[bi][:, ic * 256:(ic + 1) * 256],
                     ic == 0, ic == 7, None) for ic in range(8)]
            mm_seq(jobs, (ev_lxn, evf, rdy(("wq",))), ("qmm", t, qc))
            ev_ev = P.op("scalar", lambda qc=qc, bank=bank: sc_e.activation(
                out=kq_fm0[:, qc * 256:(qc + 1) * 256], in_=bank[:, 0:256],
                func=AF.Identity, bias=bq_sb[:, qc:qc + 1], scale=1.0),
                inc=("S", 1), waits=(rdy(("qmm", t, qc)), rdy(("kqB_used", t - 1))),
                key=("qfm", t, qc))
            rot.give(j3, ev_ev)
        for hp in (4,):
            j3, evf = rot.take()
            bank = pm[j3]
            jobs = [(bank[:, 0:256], pj_sb[:, hp * 128:(hp + 1) * 128],
                     kq_fm0[:, hp * 256:(hp + 1) * 256], True, True, None)]
            mm_seq(jobs, (rdy(("qfm", t, hp)), evf), ("qpmm", t, hp))
            P.op("scalar", lambda hp=hp, bank=bank: sc_e.activation(
                out=qp_top[0:64, hp * 256:(hp + 1) * 256], in_=bank[0:64, 0:256],
                func=AF.Relu), inc=("S", 1),
                waits=(rdy(("qpmm", t, hp)), rdy(("qp_used", t - 1)),
                       rdy(("qpz", 0)), rdy(("qpz", 1))))
            ev_ev = P.op("scalar", lambda hp=hp, bank=bank: sc_e.activation(
                out=qp_bot[64:128, hp * 256:(hp + 1) * 256], in_=bank[64:128, 0:256],
                func=AF.Relu), inc=("S", 1), key=("qp", t, hp))
            rot.give(j3, ev_ev)
        for qc in (7,):
            j3, evf = rot.take()
            bank = pm[j3]
            jobs = [(bank[:, 0:256],
                     wq_sb[:, ic * 1024 + qc * 128: ic * 1024 + (qc + 1) * 128],
                     xnfm2[bi][:, ic * 256:(ic + 1) * 256],
                     ic == 0, ic == 7, None) for ic in range(8)]
            mm_seq(jobs, (ev_lxn, evf, rdy(("wq",))), ("qmm", t, qc))
            ev_ev = P.op("scalar", lambda qc=qc, bank=bank: sc_e.activation(
                out=kq_fm0[:, qc * 256:(qc + 1) * 256], in_=bank[:, 0:256],
                func=AF.Identity, bias=bq_sb[:, qc:qc + 1], scale=1.0),
                inc=("S", 1), waits=(rdy(("qmm", t, qc)), rdy(("kqB_used", t - 1))),
                key=("qfm", t, qc))
            rot.give(j3, ev_ev)
        for hp in (5,):
            j3, evf = rot.take()
            bank = pm[j3]
            jobs = [(bank[:, 0:256], pj_sb[:, hp * 128:(hp + 1) * 128],
                     kq_fm0[:, hp * 256:(hp + 1) * 256], True, True, None)]
            mm_seq(jobs, (rdy(("qfm", t, hp)), evf), ("qpmm", t, hp))
            P.op("scalar", lambda hp=hp, bank=bank: sc_e.activation(
                out=qp_top[0:64, hp * 256:(hp + 1) * 256], in_=bank[0:64, 0:256],
                func=AF.Relu), inc=("S", 1),
                waits=(rdy(("qpmm", t, hp)), rdy(("qp_used", t - 1)),
                       rdy(("qpz", 0)), rdy(("qpz", 1))))
            ev_ev = P.op("scalar", lambda hp=hp, bank=bank: sc_e.activation(
                out=qp_bot[64:128, hp * 256:(hp + 1) * 256], in_=bank[64:128, 0:256],
                func=AF.Relu), inc=("S", 1), key=("qp", t, hp))
            rot.give(j3, ev_ev)
        for hp in (6,):
            j3, evf = rot.take()
            bank = pm[j3]
            jobs = [(bank[:, 0:256], pj_sb[:, hp * 128:(hp + 1) * 128],
                     kq_fm0[:, hp * 256:(hp + 1) * 256], True, True, None)]
            mm_seq(jobs, (rdy(("qfm", t, hp)), evf), ("qpmm", t, hp))
            P.op("scalar", lambda hp=hp, bank=bank: sc_e.activation(
                out=qp_top[0:64, hp * 256:(hp + 1) * 256], in_=bank[0:64, 0:256],
                func=AF.Relu), inc=("S", 1),
                waits=(rdy(("qpmm", t, hp)), rdy(("qp_used", t - 1)),
                       rdy(("qpz", 0)), rdy(("qpz", 1))))
            ev_ev = P.op("scalar", lambda hp=hp, bank=bank: sc_e.activation(
                out=qp_bot[64:128, hp * 256:(hp + 1) * 256], in_=bank[64:128, 0:256],
                func=AF.Relu), inc=("S", 1), key=("qp", t, hp))
            rot.give(j3, ev_ev)
        for hp in (7,):
            j3, evf = rot.take()
            bank = pm[j3]
            jobs = [(bank[:, 0:256], pj_sb[:, hp * 128:(hp + 1) * 128],
                     kq_fm0[:, hp * 256:(hp + 1) * 256], True, True, None)]
            mm_seq(jobs, (rdy(("qfm", t, hp)), evf), ("qpmm", t, hp))
            P.op("scalar", lambda hp=hp, bank=bank: sc_e.activation(
                out=qp_top[0:64, hp * 256:(hp + 1) * 256], in_=bank[0:64, 0:256],
                func=AF.Relu), inc=("S", 1),
                waits=(rdy(("qpmm", t, hp)), rdy(("qp_used", t - 1)),
                       rdy(("qpz", 0)), rdy(("qpz", 1))))
            ev_ev = P.op("scalar", lambda hp=hp, bank=bank: sc_e.activation(
                out=qp_bot[64:128, hp * 256:(hp + 1) * 256], in_=bank[64:128, 0:256],
                func=AF.Relu), inc=("S", 1), key=("qp", t, hp))
            rot.give(j3, ev_ev)
        P.r[("qpok", t)] = tuple(rdy(("qp", t, hp)) for hp in range(8))
        P.r[("kqB_used", t)] = ("T", P.cnt["T"])
        emit_chunk_at(t, bi, 0)
        emit_chunk_mid(t, bi, 0)
        for tc in range(2):
            for n in range(2):
                sl = slice(tc * 1024 + n * 512, tc * 1024 + n * 512 + 512)

                def evac_ffn(bank, sl=sl, n=n, t=t, tc=tc):
                    ev_e1 = P.op("vector", lambda sl=sl, n=n, bank=bank:
                                 vc.tensor_tensor(
                                     out=g_sb[:, sl], in0=bank[:, :],
                                     in1=sc_sb[:, n * 512:(n + 1) * 512],
                                     op=ALU.mult), inc=("V", 1),
                                 waits=(rdy(("fmm", t, tc, n)),
                                        rdy(("g_out", t - 1))))
                    ev_e2 = P.op("vector", lambda sl=sl, n=n: vc.tensor_tensor(
                        out=g_sb[:, sl], in0=g_sb[:, sl],
                        in1=fb_sb[:, n * 512:(n + 1) * 512],
                        op=ALU.add), inc=("V", 1), waits=(ev_e1,))
                    ev_e3 = P.op("vector", lambda sl=sl: vc.tensor_scalar(
                        out=g_sb[:, sl], in0=g_sb[:, sl], scalar1=100.0,
                        scalar2=-100.0, op0=ALU.min, op1=ALU.max),
                        inc=("V", 1), waits=(ev_e2,))
                    P.op("scalar", lambda sl=sl: sc_e.activation(
                        out=g_sb[:, sl], in_=g_sb[:, sl], func=AF.Silu),
                        inc=("S", 1), waits=(ev_e3,), key=("g", t, tc, n))
                    return ev_e1

                emit_flow(("fmm", t, tc, n),
                          [xnfm2[bi][:, ic * 256 + tc * 128:
                                     ic * 256 + tc * 128 + 128] for ic in range(8)],
                          [ws_sb[:, ic * 1024 + n * 512: ic * 1024 + (n + 1) * 512]
                           for ic in range(8)],
                          (ev_lxn, ev_sgn), evac_ffn)
        emit_chunk_at(t, bi, 1)
        emit_chunk_back(t, 0)
        emit_chunk_mid(t, bi, 1)
        P.r[("qp_used", t)] = ("T", P.cnt["T"])
        P.r[("kpfmB_used", t)] = ("T", P.cnt["T"])
        P.r[("ktmB_used", t)] = ("T", P.cnt["T"])
        P.r[("xnfmB_used", t)] = ("T", P.cnt["T"])
        P.r[("vaugB_used", t)] = ("T", P.cnt["T"])
        if t + 2 < b_tiles:
            emit_loads(t + 2)
        if t == 0:
            ev_tp = tap("ctxn0", [128, 2048], BF16, ctxn,
                        [rdy(("ctxn", 0, tcc, cb)) for tcc in range(2)
                         for cb in range(3)])
            if ev_tp:
                P.r[("ctxn_used", 1)] = (P.r.get(("ctxn_used", 1)), ev_tp)

    if b_tiles > 0:
        emit_chunk_back(b_tiles - 1, 1)
        emit_epilogue(b_tiles - 1)

    fq = P.cnt["YO"]
    if fq:
        P.q["sync"].append(lambda v=fq: sy.wait_ge(P.hdl["YO"], v))
    for i in range(tap_n[0]):
        v = P.cnt[f"TP{i}"]
        P.q["sync"].append(lambda i=i, v=v: sy.wait_ge(P.hdl[f"TP{i}"], v))

    with stack:
        with nc.Block() as block:
            block.sync(lambda e: [f() for f in P.q["sync"]])
            block.tensor(lambda e: [f() for f in P.q["tensor"]])
            block.vector(lambda e: [f() for f in P.q["vector"]])
            block.scalar(lambda e: [f() for f in P.q["scalar"]])
            block.gpsimd(lambda e: [f() for f in P.q["gpsimd"]])
    return nc


def prep_in_maps(L0_attn_x, L0_attn_qkv_w, L0_attn_qkv_b, L0_attn_out_w,
                 L0_attn_out_b, L0_attn_proj, L0_ffn_weights, L0_ffn_scale,
                 L0_ffn_bias):
    x = np.ascontiguousarray(np.asarray(L0_attn_x, dtype=np.float32))
    qkv_w = np.ascontiguousarray(np.asarray(L0_attn_qkv_w, dtype=np.float32))
    qkv_b = np.asarray(L0_attn_qkv_b, dtype=np.float32)
    out_w = np.ascontiguousarray(np.asarray(L0_attn_out_w, dtype=np.float32))
    out_b = np.asarray(L0_attn_out_b, dtype=np.float32)
    proj = np.asarray(L0_attn_proj, dtype=np.float32)
    fw = np.asarray(L0_ffn_weights, dtype=np.float32)
    fscale = np.asarray(L0_ffn_scale, dtype=np.float32)
    fbias = np.asarray(L0_ffn_bias, dtype=np.float32)

    projbd = np.zeros((8, 128, 128), np.float32)
    for j in range(8):
        projbd[j, 0:64, 0:64] = proj[2 * j]
        projbd[j, 64:128, 64:128] = proj[2 * j + 1]
    ii = np.arange(128)
    mask = (ii[:, None] <= ii[None, :]).astype(np.float32)
    mask4 = np.ascontiguousarray(np.tile(mask, (1, 4)))
    ident = np.eye(128, dtype=np.float32)
    bq_cols = np.ascontiguousarray(qkv_b[0:1024].reshape(8, 128).T)
    bk_cols = np.ascontiguousarray(qkv_b[1024:2048].reshape(8, 128).T)
    vbias_bc = np.ascontiguousarray(np.tile(qkv_b[2048:3072][None, :], (128, 1)))
    outb_bc = np.ascontiguousarray(np.tile(out_b[None, :], (128, 1)))
    scale_bc = np.ascontiguousarray(np.tile(fscale[None, :], (128, 1)))
    bias_bc = np.ascontiguousarray(np.tile(fbias[None, :], (128, 1)))
    ffn_wT = np.ascontiguousarray(fw.T)

    in_maps = []
    for c in range(NCORES):
        b, q = divmod(c, 4)
        rm = np.zeros((128, NCORES), np.float32)
        for qq in range(q):
            rm[:, b * 4 + qq] = 1.0
        in_maps.append({
            "x": np.ascontiguousarray(x[b, q * T:(q + 1) * T, :]),
            "qkv_w": qkv_w, "bq_cols": bq_cols, "bk_cols": bk_cols,
            "vbias_bc": vbias_bc, "out_w": out_w, "outb_bc": outb_bc,
            "projbd": projbd, "ffn_wT": ffn_wT, "scale_bc": scale_bc,
            "bias_bc": bias_bc, "mask4": mask4, "ident": ident, "rankmask": rm,
        })
    return in_maps


def kernel(**inputs):
    in_maps = prep_in_maps(**inputs)
    nc = build_program()
    res = run_bass_kernel_spmd(nc, in_maps, core_ids=list(range(NCORES)))
    y = np.empty((B, L, E), np.float32)
    for c in range(NCORES):
        b, q = divmod(c, 4)
        y[b, q * T:(q + 1) * T, :] = res.results[c]["y"]
    if DEBUG_TAPS:
        kernel.tap_results = res.results
    return y
